# version 06
# baseline (speedup 1.0000x reference)
import numpy as np
import ml_dtypes
from contextlib import ExitStack

import concourse.bass as bass
import concourse.mybir as mybir
from concourse.bass_utils import run_bass_kernel_spmd

F32 = mybir.dt.float32
BF16 = mybir.dt.bfloat16
U8 = mybir.dt.uint8
I32 = mybir.dt.int32
AF = mybir.ActivationFunctionType
ALU = mybir.AluOpType
AX = mybir.AxisListType.X

D = 4096
T = 1024
S = 4096
H = 32
II = 11008
NCH = 86
KC = 32
TCH = 8
RSQRT_DK = 0.08838834764831845
EPS = 1e-5
QUARTERS = [(0, 22), (22, 22), (44, 21), (65, 21)]


class P:

    def __init__(self, nc, taps):
        self.nc = nc
        self.st = ExitStack()
        self.taps = set(taps)
        self._semnames = set()

    def sem(self, name):
        assert name not in self._semnames
        self._semnames.add(name)
        return self.st.enter_context(self.nc.semaphore(name))

    def sb(self, name, shape, dt, stack=None):
        return (stack or self.st).enter_context(self.nc.sbuf_tensor(name, shape, dt))

    def ps(self, name, shape, dt=F32, stack=None):
        return (stack or self.st).enter_context(self.nc.psum_tensor(name, shape, dt))


def _ln_tile(v, sc, src_ap, dst_ap, c, tc, sem_v2s, sem_s2v, sq_scratch=None):
    sq = dst_ap if sq_scratch is None else sq_scratch
    v.tensor_reduce(c.aggr[:, 0:1], src_ap, AX, ALU.add)
    v.tensor_tensor(sq, src_ap, src_ap, ALU.mult)
    v.tensor_reduce(c.aggr[:, 1:2], sq, AX, ALU.add)
    v.tensor_scalar(c.negm[:], c.aggr[:, 0:1], -1.0 / D, None, ALU.mult, op1=ALU.bypass)
    v.tensor_tensor(c.mv[:, 0:1], c.negm[:], c.negm[:], ALU.mult)
    v.tensor_scalar(c.aggr[:, 1:2], c.aggr[:, 1:2], 1.0 / D, None, ALU.mult, op1=ALU.bypass)
    v.tensor_tensor(c.aggr[:, 1:2], c.aggr[:, 1:2], c.mv[:, 0:1], ALU.subtract)
    v.tensor_scalar(c.vareps[:], c.aggr[:, 1:2], EPS, None, ALU.add,
                    op1=ALU.bypass).then_inc(sem_v2s, 1)
    v.wait_ge(sem_s2v, tc + 1)
    v.tensor_scalar(dst_ap, src_ap, c.negm[:], c.rstd[:], ALU.add, op1=ALU.mult)


def _ln_scalar(sc, c, tc, sem_v2s, sem_s2v):
    sc.wait_ge(sem_v2s, tc + 1)
    sc.activation(c.rstd[:], c.vareps[:], AF.Ln)
    sc.activation(c.rstd[:], c.rstd[:], AF.Exp, scale=-0.5).then_inc(sem_s2v, 1)


def build(taps=(), stages=("s1", "s2", "s3", "s4", "s5"), level=99, skip_cc=False, dbg_unnorm=False):
    nc = bass.Bass("TRN2", target_bir_lowering=False, detect_race_conditions=False)
    p = P(nc, taps)
    stages = set(stages)

    x_in = nc.dram_tensor("x_in", [T, D], F32, kind="ExternalInput").ap()
    wqT = nc.dram_tensor("wqT", [D, D], F32, kind="ExternalInput").ap()
    wkT = nc.dram_tensor("wkT", [D, D], F32, kind="ExternalInput").ap()
    wvT = nc.dram_tensor("wvT", [D, D], F32, kind="ExternalInput").ap()
    woT = nc.dram_tensor("woT", [D, D], F32, kind="ExternalInput").ap()
    gpk = nc.dram_tensor("gpk", [NCH, 128, 512], I32, kind="ExternalInput").ap()
    upk = nc.dram_tensor("upk", [NCH, 128, 512], I32, kind="ExternalInput").ap()
    dpk = nc.dram_tensor("dpk", [II, 512], I32, kind="ExternalInput").ap()
    sg_flat = nc.dram_tensor("sg_flat", [NCH, D], F32, kind="ExternalInput").ap()
    su_flat = nc.dram_tensor("su_flat", [NCH, D], F32, kind="ExternalInput").ap()
    sg_nat = nc.dram_tensor("sg_nat", [32, II], F32, kind="ExternalInput").ap()
    su_nat = nc.dram_tensor("su_nat", [32, II], F32, kind="ExternalInput").ap()
    sd_nat = nc.dram_tensor("sd_nat", [NCH, D], F32, kind="ExternalInput").ap()
    zg = nc.dram_tensor("zg", [32, 1376], I32, kind="ExternalInput").ap()
    zu = nc.dram_tensor("zu", [32, 1376], I32, kind="ExternalInput").ap()
    zd = nc.dram_tensor("zd", [NCH, 512], I32, kind="ExternalInput").ap()
    c_id = nc.dram_tensor("c_id", [128, 128], BF16, kind="ExternalInput").ap()
    c_oneh = nc.dram_tensor("c_oneh", [128, 64], BF16, kind="ExternalInput").ap()
    c_ones1 = nc.dram_tensor("c_ones1", [1, 128], BF16, kind="ExternalInput").ap()

    out = nc.dram_tensor("out", [T, D], F32, kind="ExternalOutput").ap()

    kT_loc = nc.dram_tensor("kT_loc", [D, T], BF16)
    v_loc = nc.dram_tensor("v_loc", [T, D], BF16)
    knm_loc = nc.dram_tensor("knm_loc", [32, 1], F32)
    kT_g = nc.dram_tensor("kT_g", [4 * D, T], BF16)
    v_g = nc.dram_tensor("v_g", [S, D], BF16)
    knm_g = nc.dram_tensor("knm_g", [128, 1], F32)
    qT_d = nc.dram_tensor("qT_d", [D, T], BF16)
    negb_d = nc.dram_tensor("negb_d", [32, 1], F32)
    attnT_d = nc.dram_tensor("attnT_d", [D, T], BF16)
    attno_d = nc.dram_tensor("attno_d", [T, D], F32)
    xp_d = nc.dram_tensor("xp_d", [T, D], F32)
    sgf_bf = nc.dram_tensor("sgf_bf", [NCH, D], BF16)
    suf_bf = nc.dram_tensor("suf_bf", [NCH, D], BF16)
    sdf_bf = nc.dram_tensor("sdf_bf", [NCH, D], BF16)
    zs_g_d = nc.dram_tensor("zs_g_d", [32, II], BF16)
    zs_u_d = nc.dram_tensor("zs_u_d", [32, II], BF16)
    zs_d_d = nc.dram_tensor("zs_d_d", [NCH, D], BF16)
    wd_bf = nc.dram_tensor("wd_bf", [NCH, 128, D], BF16)
    op_d = nc.dram_tensor("op_d", [T, D], BF16)

    def tap(name, shape, dt):
        if name in p.taps:
            return nc.dram_tensor("tap_" + name, shape, dt, kind="ExternalOutput").ap()
        return None

    tap_xln1 = tap("xln1", [T, D], BF16)
    tap_negb = tap("negb", [32, T], BF16)
    tap_probs = tap("probs", [S, 512], BF16)
    tap_kTg = tap("kTg", [4 * D, T], BF16)
    tap_vloc = tap("vloc", [T, D], BF16)
    tap_vg = tap("vg", [S, D], BF16)
    tap_qT = tap("qT", [D, T], BF16)

    ident = p.sb("ident", [128, 128], BF16)
    oneh = p.sb("oneh", [128, 64], BF16)
    ones1 = p.sb("ones1", [1, 128], BF16)
    negb = p.sb("negb", [32, T], BF16)
    xgT = p.sb("xgT", [32, T], BF16)
    knmx = p.sb("knmx", [32, 1], F32)
    xln3T = p.sb("xln3T", [128, KC * T], BF16)

    cc_sem = p.sem("cc")
    CC = {"n": 0}

    cst = p.sem("cst")
    with nc.Block() as blk:
        @blk.sync
        def _(sync):
            sync.dma_start(out=ident[:], in_=c_id).then_inc(cst, 16)
            sync.dma_start(out=oneh[:], in_=c_oneh).then_inc(cst, 16)
            sync.dma_start(out=ones1[:], in_=c_ones1).then_inc(cst, 16)
            sync.wait_ge(cst, 48)

    _sid1, _ = nc.enter_named_scope("s1", False)
    if "s1" in stages:
        s1 = ExitStack()
        xln1T = p.sb("xln1T", [128, KC * T], BF16, s1)

        class LNC:
            pass
        lc = LNC()
        lc.mv = p.sb("ln_mv", [128, 48], F32, s1)
        lc.aggr = p.sb("ln_ag", [128, 2], F32, s1)
        lc.negm = p.sb("ln_nm", [128, 1], F32, s1)
        lc.vareps = p.sb("ln_ve", [128, 1], F32, s1)
        lc.rstd = p.sb("ln_rs", [128, 1], F32, s1)
        lnq = p.sb("lnq", [32, T], F32, s1)
        halflnk = p.sb("halflnk", [32, 1], F32, s1)
        s1a = ExitStack()
        xbuf = [p.sb(f"xbuf{i}", [128, D], F32, s1a) for i in range(2)]
        xls = [p.sb(f"xls{i}", [128, D], BF16, s1a) for i in range(2)]
        ptA = p.ps("ptA", [128, 512], BF16, s1)
        ptB = p.ps("ptB", [128, 512], BF16, s1)

        x_ld = p.sem("x_ld"); x_fr = p.sem("x_fr"); xn = p.sem("xn")
        tapx = p.sem("tapx") if tap_xln1 is not None else None
        v2s = p.sem("s1_v2s"); s2v = p.sem("s1_s2v")
        pe_t = p.sem("pe_t"); ev2 = p.sem("ev2")

        with nc.Block() as blk:
            @blk.sync
            def _(sync):
                for tc in range(TCH):
                    if tap_xln1 is not None and tc >= 2:
                        tp = tc - 2
                        sync.wait_ge(xn, tp + 1)
                        sync.dma_start(out=tap_xln1[tp * 128:(tp + 1) * 128, :],
                                       in_=xls[tp % 2][:]).then_inc(tapx, 16)
                    if tc >= 2:
                        sync.wait_ge(x_fr, tc - 1)
                    sync.dma_start(out=xbuf[tc % 2][:],
                                   in_=x_in[tc * 128:(tc + 1) * 128, :]).then_inc(x_ld, 16)
                if tap_xln1 is not None:
                    for tp in range(TCH - 2, TCH):
                        sync.wait_ge(xn, tp + 1)
                        sync.dma_start(out=tap_xln1[tp * 128:(tp + 1) * 128, :],
                                       in_=xls[tp % 2][:]).then_inc(tapx, 16)

            @blk.vector
            def _(v):
                for tc in range(TCH):
                    v.wait_ge(x_ld, 16 * (tc + 1))
                    if tc >= 2:
                        v.wait_ge(pe_t, (tc - 1) * 8)
                        if tap_xln1 is not None:
                            v.wait_ge(tapx, 16 * (tc - 1))
                    _ln_tile(v, None, xbuf[tc % 2][:], xls[tc % 2][:], lc, tc, v2s, s2v)
                    v.engine_nop().then_inc(x_fr, 1)
                    v.engine_nop().then_inc(xn, 1)

            @blk.scalar
            def _(sc):
                for tc in range(TCH):
                    _ln_scalar(sc, lc, tc, v2s, s2v)
                    for g in range(8):
                        gi = tc * 8 + g
                        pt = ptA if g % 2 == 0 else ptB
                        sc.wait_ge(pe_t, gi + 1)
                        for j in range(4):
                            dc = g * 4 + j
                            ins = sc.copy(xln1T[:, (dc * 8 + tc) * 128:(dc * 8 + tc + 1) * 128],
                                          pt[:, j * 128:(j + 1) * 128])
                        ins.then_inc(ev2, 1)

            @blk.tensor
            def _(t):
                for tc in range(TCH):
                    t.wait_ge(xn, tc + 1)
                    for g in range(8):
                        gi = tc * 8 + g
                        pt = ptA if g % 2 == 0 else ptB
                        if gi >= 2:
                            t.wait_ge(ev2, gi - 1)
                        for j in range(4):
                            dc = g * 4 + j
                            ins = t.transpose(pt[:, j * 128:(j + 1) * 128],
                                              xls[tc % 2][:, dc * 128:(dc + 1) * 128], ident[:])
                            if j == 3:
                                ins.then_inc(pe_t, 1)

        s1a.close()
        if level < 2:
            return nc
        s1b = ExitStack()
        wslab = [p.sb(f"wslab{i}", [128, KC * 128], BF16, s1b) for i in range(2)]
        kst = [p.sb(f"kst{i}", [128, T], BF16, s1b) for i in range(2)]
        sqst = [p.sb(f"sqst{i}", [128, 512], BF16, s1b) for i in range(2)]
        pk = [p.ps(f"pk{i}", [128, 512], F32, s1) for i in range(2)]
        pkn = p.ps("pkn", [32, T], F32, s1)
        pqn = p.ps("pqn", [32, T], F32, s1)

        def proj_feature_major(tag, w_src, out_dram, scale, pnorm):
            w_ld = p.sem(tag + "_wld"); w_fr = p.sem(tag + "_wfr")
            pe_m = p.sem(tag + "_pe"); ev_m = p.sem(tag + "_ev")
            sq_m = p.sem(tag + "_sq"); nrm = p.sem(tag + "_nrm"); sp_m = p.sem(tag + "_sp")

            with nc.Block() as blk:
                @blk.gpsimd
                def _(g):
                    for ec in range(KC):
                        if ec >= 2:
                            g.wait_ge(w_fr, ec - 1)
                        g.dma_start(
                            out=wslab[ec % 2][:].rearrange("p (dc e) -> p dc e", e=128),
                            in_=w_src[:, ec * 128:(ec + 1) * 128]
                            .rearrange("(dc p) e -> p dc e", p=128)).then_inc(w_ld, 16)

                @blk.tensor
                def _(t):
                    for ec in range(KC):
                        t.wait_ge(w_ld, 16 * (ec + 1))
                        for th in range(2):
                            idx = ec * 2 + th
                            if idx >= 2:
                                t.wait_ge(ev_m, idx - 1)
                            for dc in range(KC):
                                ins = t.matmul(
                                    pk[idx % 2][:],
                                    wslab[ec % 2][:, dc * 128:(dc + 1) * 128],
                                    xln1T[:, (dc * 8 + th * 4) * 128:(dc * 8 + th * 4 + 4) * 128],
                                    start=(dc == 0), stop=(dc == KC - 1))
                            ins.then_inc(pe_m, 1)
                            if th == 1:
                                last_slab_use = ins
                            if idx >= 1:
                                pidx = idx - 1
                                pec, pth = pidx // 2, pidx % 2
                                t.wait_ge(sq_m, pidx + 1)
                                t.matmul(pnorm[:, pth * 512:(pth + 1) * 512],
                                         oneh[:, 31 - pec:63 - pec], sqst[pidx % 2][:],
                                         start=(pec == 0), stop=(pec == KC - 1)).then_inc(nrm, 1)
                        t.wait_ge(pe_m, 2 * ec + 2)
                        t.nop().then_inc(w_fr, 1)
                    pidx = 2 * KC - 1
                    pec, pth = pidx // 2, pidx % 2
                    t.wait_ge(sq_m, pidx + 1)
                    t.matmul(pnorm[:, pth * 512:(pth + 1) * 512],
                             oneh[:, 31 - pec:63 - pec], sqst[pidx % 2][:],
                             start=(pec == 0), stop=(pec == KC - 1)).then_inc(nrm, 1)

                @blk.vector
                def _(v):
                    for idx in range(2 * KC):
                        ec, th = idx // 2, idx % 2
                        v.wait_ge(pe_m, idx + 1)
                        if th == 0 and ec >= 2:
                            v.wait_ge(sp_m, 16 * (ec - 1))
                        v.tensor_scalar(kst[ec % 2][:, th * 512:(th + 1) * 512],
                                        pk[idx % 2][:], scale, None, ALU.mult, op1=ALU.bypass)
                        v.engine_nop().then_inc(ev_m, 1)
                        if idx >= 2:
                            v.wait_ge(nrm, idx - 1)
                        v.tensor_tensor(sqst[idx % 2][:],
                                        kst[ec % 2][:, th * 512:(th + 1) * 512],
                                        kst[ec % 2][:, th * 512:(th + 1) * 512],
                                        ALU.mult).then_inc(sq_m, 1)

                @blk.sync
                def _(sync):
                    for ec in range(KC):
                        sync.wait_ge(sq_m, 2 * ec + 2)
                        sync.dma_start(out=out_dram[ec * 128:(ec + 1) * 128, :],
                                       in_=kst[ec % 2][:]).then_inc(sp_m, 16)
                    sync.wait_ge(sp_m, 16 * KC)
            return nrm, sp_m

        knrm, ksp = proj_feature_major("K", wkT, kT_loc.ap(), 1.0, pkn)
        if level < 3:
            return nc
        knm_dma = p.sem("knm_dma")
        knm_r = p.sem("knm_r")
        with nc.Block() as blk:
            @blk.vector
            def _(v):
                v.tensor_reduce(knmx[:], pkn[:], AX, ALU.max).then_inc(knm_r, 1)

            @blk.sync
            def _(sync):
                sync.wait_ge(knm_r, 1)
                sync.dma_start(out=knm_loc[:, :], in_=knmx[:]).then_inc(knm_dma, 16)

        if level < 4:
            return nc
        qnrm, qsp = proj_feature_major("Q", wqT, qT_d.ap(), RSQRT_DK, pqn)
        if level < 5:
            return nc

        s1b.close()
        vslab = [p.sb(f"vslab{i}", [128, KC * 512], BF16, s1) for i in range(2)]
        vst = [p.sb(f"vst{i}", [128, 512], BF16, s1) for i in range(2)]
        v_ld = p.sem("v_ld"); v_fr = p.sem("v_fr"); pe_v = p.sem("pe_v")
        vev = p.sem("vev"); vsp = p.sem("vsp")

        with nc.Block() as blk:
            @blk.gpsimd
            def _(g):
                for eh in range(TCH):
                    if eh >= 2:
                        g.wait_ge(v_fr, eh - 1)
                    g.dma_start(out=vslab[eh % 2][:].rearrange("p (dc e) -> p dc e", e=512),
                                in_=wvT[:, eh * 512:(eh + 1) * 512]
                                .rearrange("(dc p) e -> p dc e", p=128)).then_inc(v_ld, 16)
                g.wait_ge(vsp, 16 * 64)

            @blk.tensor
            def _(t):
                for eh in range(TCH):
                    t.wait_ge(v_ld, 16 * (eh + 1))
                    for tc in range(TCH):
                        idx = eh * 8 + tc
                        if idx >= 2:
                            t.wait_ge(vev, idx - 1)
                        for dc in range(KC):
                            ins = t.matmul(pk[idx % 2][:],
                                           xln1T[:, (dc * 8 + tc) * 128:(dc * 8 + tc + 1) * 128],
                                           vslab[eh % 2][:, dc * 512:(dc + 1) * 512],
                                           start=(dc == 0), stop=(dc == KC - 1))
                        ins.then_inc(pe_v, 1)
                    t.wait_ge(pe_v, 8 * (eh + 1))
                    t.nop().then_inc(v_fr, 1)

            @blk.vector
            def _(v):
                for idx in range(64):
                    v.wait_ge(pe_v, idx + 1)
                    if idx >= 2:
                        v.wait_ge(vsp, 16 * (idx - 1))
                    v.tensor_copy(vst[idx % 2][:], pk[idx % 2][:])
                    v.engine_nop().then_inc(vev, 1)

            @blk.sync
            def _(sync):
                for idx in range(64):
                    eh, tc = idx // 8, idx % 8
                    sync.wait_ge(vev, idx + 1)
                    sync.dma_start(out=v_loc.ap()[tc * 128:(tc + 1) * 128,
                                                  eh * 512:(eh + 1) * 512],
                                   in_=vst[idx % 2][:]).then_inc(vsp, 16)
                sync.wait_ge(vsp, 16 * 64)

        if not skip_cc:
            with nc.Block() as blk:
                @blk.tensor
                def _(t):
                    for i in range(2000):
                        t.matmul(pk[0][:, 0:128], ident[:], ident[:],
                                 start=(i == 0), stop=(i == 1999))

                @blk.gpsimd
                def _(g):
                    g.wait_ge(knm_dma, 16)
                    g.collective_compute("AllGather", ALU.bypass,
                                         replica_groups=[[0, 1, 2, 3], [4, 5, 6, 7]],
                                         ins=[kT_loc.ap().opt()],
                                         outs=[kT_g.ap().opt()]).then_inc(cc_sem, 1)
                    g.collective_compute("AllGather", ALU.bypass,
                                         replica_groups=[[0, 1, 2, 3], [4, 5, 6, 7]],
                                         ins=[knm_loc.ap().opt()],
                                         outs=[knm_g.ap().opt()]).then_inc(cc_sem, 1)
                    g.collective_compute("AllGather", ALU.bypass,
                                         replica_groups=[[0, 1, 2, 3], [4, 5, 6, 7]],
                                         ins=[v_loc.ap().opt()],
                                         outs=[v_g.ap().opt()]).then_inc(cc_sem, 1)
                    g.wait_ge(cc_sem, 3)
                    CC["n"] += 3

        if tap_vloc is not None:
            tvl = p.sem("tvl")
            with nc.Block() as blk:
                @blk.sync
                def _(sync):
                    sync.dma_start(out=tap_vloc, in_=v_loc.ap()).then_inc(tvl, 16)
                    sync.wait_ge(tvl, 16)

        if level < 6:
            return nc
        qsqm = p.sb("qsqm", [32, 1], F32, s1)
        bprep = p.sem("bprep")
        with nc.Block() as blk:
            @blk.sync
            def _(sync):
                sync.wait_ge(cc_sem, 2)
                sync.dma_start(out=knmx[:], in_=knm_g.ap()[0:32, :]).then_inc(bprep, 16)
                sync.dma_start(out=lc.mv[0:32, 0:1], in_=knm_g.ap()[32:64, :]).then_inc(bprep, 16)
                sync.dma_start(out=lc.mv[0:32, 1:2], in_=knm_g.ap()[64:96, :]).then_inc(bprep, 16)
                sync.dma_start(out=lc.mv[0:32, 2:3], in_=knm_g.ap()[96:128, :]).then_inc(bprep, 16)
                sync.wait_ge(bprep, 64)

        with nc.Block() as blk:
            @blk.vector
            def _(v):
                v.tensor_tensor(knmx[:], knmx[:], lc.mv[0:32, 0:1], ALU.max)
                v.tensor_tensor(knmx[:], knmx[:], lc.mv[0:32, 1:2], ALU.max)
                v.tensor_tensor(knmx[:], knmx[:], lc.mv[0:32, 2:3], ALU.max)

        with nc.Block() as blk:
            @blk.vector
            def _(v):
                v.tensor_reduce(qsqm[:], pqn[:], AX, ALU.max)

        with nc.Block() as blk:
            @blk.scalar
            def _(sc):
                sc.activation(halflnk[:], knmx[:], AF.Ln)
                sc.activation(lnq[0:32, 0:1], qsqm[:], AF.Ln)

        with nc.Block() as blk:
            @blk.vector
            def _(v):
                v.tensor_scalar(halflnk[:], halflnk[:], 0.5, None, ALU.mult, op1=ALU.bypass)

        with nc.Block() as blk:
            @blk.scalar
            def _(sc):
                sc.activation(lnq[0:32, 0:1], lnq[0:32, 0:1], AF.Exp, scale=0.5, bias=halflnk[:])

        nbspill = p.sem("nbspill")
        with nc.Block() as blk:
            @blk.vector
            def _(v):
                v.tensor_scalar(qsqm[:], lnq[0:32, 0:1], -1.0, -0.125, ALU.mult, op1=ALU.add)

        with nc.Block() as blk:
            @blk.sync
            def _(sync):
                sync.dma_start(out=negb_d.ap(), in_=qsqm[:]).then_inc(nbspill, 16)
                sync.wait_ge(nbspill, 16)

        if tap_kTg is not None or tap_vg is not None or tap_qT is not None:
            tkg = p.sem("tkg")
            with nc.Block() as blk:
                @blk.sync
                def _(sync):
                    sync.wait_ge(cc_sem, 3)
                    n = 0
                    for t_, s_ in [(tap_kTg, kT_g), (tap_vg, v_g), (tap_qT, qT_d)]:
                        if t_ is not None:
                            sync.dma_start(out=t_, in_=s_.ap()).then_inc(tkg, 16)
                            n += 16
                    sync.wait_ge(tkg, n)

        if tap_negb is not None:
            tapnb = p.sem("tapnb")
            with nc.Block() as blk:
                @blk.sync
                def _(sync):
                    sync.dma_start(out=tap_negb, in_=negb[:]).then_inc(tapnb, 16)
                    sync.wait_ge(tapnb, 16)

        s1.close()

    nc.leave_named_scope("s1", _sid1, False)
    _sid2, _ = nc.enter_named_scope("s2", False)
    if "s2" in stages and level >= 7:
        s2 = ExitStack()
        kT_t = [p.sb(f"s2kt{i}", [128, S], BF16, s2) for i in range(2)]
        v_t = [p.sb(f"s2vt{i}", [128, KC * 144], BF16, s2) for i in range(2)]
        q_t = [p.sb(f"s2qt{i}", [128, T], BF16, s2) for i in range(2)]
        nbc = [p.sb(f"s2nbc{i}", [128, 1], F32, s2) for i in range(2)]
        slab = [p.sb(f"s2sl{i}", [128, KC * 512], BF16, s2) for i in range(2)]
        ao_st = [p.sb(f"s2ao{i}", [128, 128], BF16, s2) for i in range(2)]
        at_st = [p.sb(f"s2at{i}", [128, T], BF16, s2) for i in range(2)]
        psc = [p.ps(f"s2psc{i}", [128, 1024], F32, s2) for i in range(2)]
        pav = [p.ps(f"s2pav{i}", [128, 132], F32, s2) for i in range(2)]
        ptr = [p.ps(f"s2ptr{i}", [128, 512], BF16, s2) for i in range(2)]
        rcp2 = [p.sb(f"s2rcp{i}", [128, 1], F32, s2) for i in range(2)]
        lnd = p.sb("s2lnd", [128, 1], F32, s2)

        s2_pf = p.sem("s2_pf"); s2_sc = p.sem("s2_sc"); s2_exp = p.sem("s2_exp")
        s2_avp = p.sem("s2_avp"); s2_ave = p.sem("s2_ave"); s2_tr = p.sem("s2_tr")
        s2_tre = p.sem("s2_tre"); s2_asp = p.sem("s2_asp"); s2_rc = p.sem("s2_rc")

        NU = 2 * H

        with nc.Block() as blk:
            @blk.gpsimd
            def _(g):
                g.memset(v_t[0][:].rearrange("p (kc w) -> p kc w", w=144)[:, :, 128:129], 1.0)
                g.memset(v_t[1][:].rearrange("p (kc w) -> p kc w", w=144)[:, :, 128:129], 1.0)

        def pe_scores(t, u):
            h, qh = u // 2, u % 2
            t.wait_ge(s2_pf, 112 * (h + 1))
            for kc in range(KC):
                g2 = u * 16 + kc // 2
                if kc % 2 == 0 and g2 >= 2:
                    t.wait_ge(s2_exp, g2 - 1)
                pb = psc[g2 % 2][:, (kc % 2) * 512:(kc % 2 + 1) * 512]
                t.matmul(pb, kT_t[h % 2][:, kc * 128:(kc + 1) * 128],
                         q_t[h % 2][:, qh * 512:(qh + 1) * 512],
                         start=True, stop=True).then_inc(s2_sc, 1)

        def pe_avtrans(t, u):
            h, qh = u // 2, u % 2
            t.wait_ge(s2_exp, 16 * (u + 1))
            for qc in range(4):
                a = 4 * u + qc
                if a >= 2:
                    t.wait_ge(s2_ave, a - 1)
                for kc in range(KC):
                    ins = t.matmul(pav[qc % 2][:, 0:129],
                                   slab[u % 2][:, kc * 512 + qc * 128:kc * 512 + (qc + 1) * 128],
                                   v_t[h % 2][:, kc * 144:kc * 144 + 129],
                                   start=(kc == 0), stop=(kc == KC - 1))
                ins.then_inc(s2_avp, 1)
                if a >= 1:
                    pa = a - 1
                    pu, pqc = pa // 4, pa % 4
                    t.wait_ge(s2_ave, pa + 1)
                    if pu >= 2 and pqc == 0:
                        t.wait_ge(s2_tre, pu - 1)
                    t.transpose(ptr[pu % 2][:, pqc * 128:(pqc + 1) * 128],
                                ao_st[pa % 2][:], ident[:]).then_inc(s2_tr, 1)

        with nc.Block() as blk:
            @blk.sync
            def _(sync):
                for h in range(H):
                    if h >= 2:
                        sync.wait_ge(s2_asp, 16 * (h - 1))
                    for r in range(4):
                        sync.dma_start(
                            out=kT_t[h % 2][:, r * T:(r + 1) * T],
                            in_=kT_g.ap()[r * D + h * 128:r * D + (h + 1) * 128, :]
                        ).then_inc(s2_pf, 16)
                    sync.dma_start(
                        out=v_t[h % 2][:].rearrange("p (kc w) -> p kc w", w=144)[:, :, 0:128],
                        in_=v_g.ap()[:, h * 128:(h + 1) * 128]
                        .rearrange("(kc p) e -> p kc e", p=128)).then_inc(s2_pf, 16)
                    sync.dma_start(out=q_t[h % 2][:],
                                   in_=qT_d.ap()[h * 128:(h + 1) * 128, :]).then_inc(s2_pf, 16)
                    sync.dma_start(out=nbc[h % 2][:],
                                   in_=negb_d.ap()[h:h + 1, :].to_broadcast((128, 1))).then_inc(s2_pf, 16)
                    if h >= 1:
                        ph = h - 1
                        sync.wait_ge(s2_tre, 2 * (ph + 1))
                        sync.dma_start(out=attnT_d.ap()[ph * 128:(ph + 1) * 128, :],
                                       in_=at_st[ph % 2][:]).then_inc(s2_asp, 16)
                ph = H - 1
                sync.wait_ge(s2_tre, 2 * (ph + 1))
                sync.dma_start(out=attnT_d.ap()[ph * 128:(ph + 1) * 128, :],
                               in_=at_st[ph % 2][:]).then_inc(s2_asp, 16)
                sync.wait_ge(s2_asp, 16 * H)

            @blk.tensor
            def _(t):
                for u in range(NU):
                    pe_scores(t, u)
                    if u >= 1:
                        pe_avtrans(t, u - 1)
                pe_avtrans(t, NU - 1)
                a = 4 * NU - 1
                t.wait_ge(s2_ave, a + 1)
                t.transpose(ptr[(a // 4) % 2][:, (a % 4) * 128:(a % 4 + 1) * 128],
                            ao_st[a % 2][:], ident[:]).then_inc(s2_tr, 1)

            def rcp_pair(sc, a):
                sc.wait_ge(s2_avp, a + 1)
                if a >= 2:
                    sc.wait_ge(s2_ave, a - 1)
                sc.activation(lnd[:], pav[(a % 4) % 2][:, 128:129], AF.Ln)
                sc.activation(rcp2[a % 2][:], lnd[:], AF.Exp,
                              scale=-1.0).then_inc(s2_rc, 1)

            @blk.scalar
            def _(sc):
                for u in range(NU):
                    h = u // 2
                    if u >= 2:
                        sc.wait_ge(s2_ave, 4 * (u - 1))
                    for t2 in range(16):
                        g2 = u * 16 + t2
                        sc.wait_ge(s2_sc, u * 32 + 2 * t2 + 2)
                        sc.activation(slab[u % 2][:, t2 * 1024:(t2 + 1) * 1024],
                                      psc[g2 % 2][:], AF.Exp,
                                      bias=nbc[h % 2][:]).then_inc(s2_exp, 1)
                    if u >= 1:
                        for qc in range(4):
                            rcp_pair(sc, 4 * (u - 1) + qc)
                for qc in range(4):
                    rcp_pair(sc, 4 * (NU - 1) + qc)

            @blk.vector
            def _(v):
                for u in range(NU):
                    h, qh = u // 2, u % 2
                    for qc in range(4):
                        a = 4 * u + qc
                        v.wait_ge(s2_avp, a + 1)
                        if a >= 2:
                            v.wait_ge(s2_tr, a - 1)
                        if dbg_unnorm == "den":
                            v.tensor_copy(ao_st[a % 2][:],
                                          pav[qc % 2][:, 1:129]).then_inc(s2_ave, 1)
                        elif dbg_unnorm:
                            v.tensor_copy(ao_st[a % 2][:],
                                          pav[qc % 2][:, 0:128]).then_inc(s2_ave, 1)
                        else:
                            v.wait_ge(s2_rc, a + 1)
                            v.tensor_scalar(ao_st[a % 2][:], pav[qc % 2][:, 0:128],
                                            rcp2[a % 2][:], None, ALU.mult,
                                            op1=ALU.bypass).then_inc(s2_ave, 1)
                    if u >= 1:
                        pu = u - 1
                        ph, pqh = pu // 2, pu % 2
                        v.wait_ge(s2_tr, 4 * (pu + 1))
                        if ph >= 2 and pqh == 0:
                            v.wait_ge(s2_asp, 16 * (ph - 1))
                        v.tensor_copy(at_st[ph % 2][:, pqh * 512:(pqh + 1) * 512],
                                      ptr[pu % 2][:]).then_inc(s2_tre, 1)
                pu = NU - 1
                ph, pqh = pu // 2, pu % 2
                v.wait_ge(s2_tr, 4 * (pu + 1))
                v.tensor_copy(at_st[ph % 2][:, pqh * 512:(pqh + 1) * 512],
                              ptr[pu % 2][:]).then_inc(s2_tre, 1)

        if "slab0" in p.taps:
            tap_slab0 = nc.dram_tensor("tap_slab0", [128, KC * 512], BF16, kind="ExternalOutput").ap()
            tap_vt0 = nc.dram_tensor("tap_vt0", [128, KC * 144], BF16, kind="ExternalOutput").ap()
            tap_nb0 = nc.dram_tensor("tap_nb0", [1, T], BF16, kind="ExternalOutput").ap()
            tsl = p.sem("tsl")
            with nc.Block() as blk:
                @blk.sync
                def _(sync):
                    sync.dma_start(out=tap_slab0, in_=slab[0][:]).then_inc(tsl, 16)
                    sync.dma_start(out=tap_vt0, in_=v_t[0][:]).then_inc(tsl, 16)
                    sync.dma_start(out=tap_nb0, in_=nb_st[0][:]).then_inc(tsl, 16)
                    sync.wait_ge(tsl, 48)

        if "attnT" in p.taps:
            tap_attnT = nc.dram_tensor("tap_attnT", [D, T], BF16, kind="ExternalOutput").ap()
            tat = p.sem("tat")
            with nc.Block() as blk:
                @blk.sync
                def _(sync):
                    sync.dma_start(out=tap_attnT, in_=attnT_d.ap()).then_inc(tat, 16)
                    sync.wait_ge(tat, 16)

        s2.close()

    nc.leave_named_scope("s2", _sid2, False)
    _sid3, _ = nc.enter_named_scope("s3", False)
    if "s3" in stages and level >= 8:
        s3a = ExitStack()
        woslab = [p.sb(f"s3wo{i}", [128, KC * 512], BF16, s3a) for i in range(2)]
        atile = p.sb("s3at", [128, TCH * KC * 128], BF16, s3a)
        ost = [p.sb(f"s3os{i}", [128, 512], F32, s3a) for i in range(2)]
        po = [p.ps(f"s3po{i}", [128, 512], F32, s3a) for i in range(2)]
        s3_wl = p.sem("s3_wl"); s3_al = p.sem("s3_al"); s3_pe = p.sem("s3_pe")
        s3_ev = p.sem("s3_ev"); s3_sp = p.sem("s3_sp")

        with nc.Block() as blk:
            @blk.sync
            def _(sync):
                sync.dma_start(
                    out=atile[:].rearrange("p (tc ec t) -> p tc ec t", ec=KC, t=128),
                    in_=attnT_d.ap().rearrange("(ec p) (tc t) -> p tc ec t", p=128, t=128)
                ).then_inc(s3_al, 16)

            @blk.gpsimd
            def _(g):
                for ob in range(TCH):
                    if ob >= 2:
                        g.wait_ge(s3_pe, 8 * (ob - 1))
                    g.dma_start(
                        out=woslab[ob % 2][:].rearrange("p (ec o) -> p ec o", o=512),
                        in_=woT[:, ob * 512:(ob + 1) * 512]
                        .rearrange("(ec p) o -> p ec o", p=128)).then_inc(s3_wl, 16)

            @blk.tensor
            def _(t):
                t.wait_ge(s3_al, 16)
                for ob in range(TCH):
                    t.wait_ge(s3_wl, 16 * (ob + 1))
                    for tc in range(TCH):
                        idx = ob * 8 + tc
                        if idx >= 2:
                            t.wait_ge(s3_ev, idx - 1)
                        for ec in range(KC):
                            ins = t.matmul(po[idx % 2][:],
                                           atile[:, (tc * KC + ec) * 128:(tc * KC + ec + 1) * 128],
                                           woslab[ob % 2][:, ec * 512:(ec + 1) * 512],
                                           start=(ec == 0), stop=(ec == KC - 1))
                        ins.then_inc(s3_pe, 1)

            @blk.vector
            def _(v):
                for idx in range(64):
                    v.wait_ge(s3_pe, idx + 1)
                    if idx >= 2:
                        v.wait_ge(s3_sp, 16 * (idx - 1))
                    v.tensor_copy(ost[idx % 2][:], po[idx % 2][:]).then_inc(s3_ev, 1)

            @blk.scalar
            def _(sc):
                for idx in range(64):
                    ob, tc = idx // 8, idx % 8
                    sc.wait_ge(s3_ev, idx + 1)
                    sc.dma_start(out=attno_d.ap()[tc * 128:(tc + 1) * 128,
                                                  ob * 512:(ob + 1) * 512],
                                 in_=ost[idx % 2][:]).then_inc(s3_sp, 16)
                sc.wait_ge(s3_sp, 16 * 64)
        s3a.close()

        s3b = ExitStack()
        aor = [p.sb(f"s3ao{i}", [128, D], F32, s3b) for i in range(2)]
        xr = [p.sb(f"s3xr{i}", [128, D], F32, s3b) for i in range(2)]
        x3 = [p.sb(f"s3x3{i}", [128, D], BF16, s3b) for i in range(2)]
        sq3 = p.sb("s3sq", [128, D], F32, s3b)
        xgf = p.sb("s3xgf", [128, 32], F32, s3b)
        xg = p.sb("s3xg", [128, 32], BF16, s3b)

        class LN3C:
            pass
        l3 = LN3C()
        l3.aggr = p.sb("s3ag", [128, 2], F32, s3b)
        l3.negm = p.sb("s3nm", [128, 1], F32, s3b)
        l3.vareps = p.sb("s3ve", [128, 1], F32, s3b)
        l3.rstd = p.sb("s3rs", [128, 1], F32, s3b)
        l3.mv = p.sb("s3mv", [128, 2], F32, s3b)
        pt3 = [p.ps(f"s3pt{i}", [128, 512], BF16, s3b) for i in range(2)]
        ptg = p.ps("s3ptg", [32, 128], BF16, s3b)

        s3_ld = p.sem("s3_ld"); s3_v2s = p.sem("s3_v2s"); s3_s2v = p.sem("s3_s2v")
        s3_x3 = p.sem("s3_x3"); s3_tp = p.sem("s3_tp"); s3_te = p.sem("s3_te")
        s3_xp = p.sem("s3_xp"); s3_tg = p.sem("s3_tg"); s3_tge = p.sem("s3_tge")
        s3_tgc = p.sem("s3_tgc")

        with nc.Block() as blk:
            @blk.sync
            def _(sync):
                for tc in range(TCH):
                    if tc >= 2:
                        sync.wait_ge(s3_x3, tc - 1)
                    sync.dma_start(out=aor[tc % 2][:],
                                   in_=attno_d.ap()[tc * 128:(tc + 1) * 128, :]).then_inc(s3_ld, 16)
                    sync.dma_start(out=xr[tc % 2][:],
                                   in_=x_in[tc * 128:(tc + 1) * 128, :]).then_inc(s3_ld, 16)
                    sync.wait_ge(s3_xp, tc + 1)
                    sync.dma_start(out=xp_d.ap()[tc * 128:(tc + 1) * 128, :],
                                   in_=aor[tc % 2][:]).then_inc(s3_ld, 16)

            @blk.vector
            def _(v):
                for tc in range(TCH):
                    v.wait_ge(s3_ld, 48 * tc + 32)
                    v.tensor_reduce(l3.aggr[:, 0:1], aor[tc % 2][:], AX, ALU.add)
                    v.tensor_tensor(sq3[:], aor[tc % 2][:], aor[tc % 2][:], ALU.mult)
                    v.tensor_reduce(l3.aggr[:, 1:2], sq3[:], AX, ALU.add)
                    v.tensor_scalar(l3.negm[:], l3.aggr[:, 0:1], -1.0 / D, None, ALU.mult, op1=ALU.bypass)
                    v.tensor_tensor(l3.mv[:, 0:1], l3.negm[:], l3.negm[:], ALU.mult)
                    v.tensor_scalar(l3.aggr[:, 1:2], l3.aggr[:, 1:2], 1.0 / D, None, ALU.mult, op1=ALU.bypass)
                    v.tensor_tensor(l3.aggr[:, 1:2], l3.aggr[:, 1:2], l3.mv[:, 0:1], ALU.subtract)
                    v.tensor_scalar(l3.vareps[:], l3.aggr[:, 1:2], EPS, None, ALU.add,
                                    op1=ALU.bypass).then_inc(s3_v2s, 1)
                    v.wait_ge(s3_s2v, 2 * tc + 1)
                    v.tensor_scalar(sq3[:], aor[tc % 2][:], l3.negm[:], l3.rstd[:], ALU.add, op1=ALU.mult)
                    v.tensor_tensor(aor[tc % 2][:], xr[tc % 2][:], sq3[:],
                                    ALU.add).then_inc(s3_xp, 1)
                    v.tensor_reduce(l3.aggr[:, 0:1], aor[tc % 2][:], AX, ALU.add)
                    v.tensor_tensor(sq3[:], aor[tc % 2][:], aor[tc % 2][:], ALU.mult)
                    v.tensor_reduce(l3.aggr[:, 1:2], sq3[:], AX, ALU.add)
                    v.tensor_scalar(l3.negm[:], l3.aggr[:, 0:1], -1.0 / D, None, ALU.mult, op1=ALU.bypass)
                    v.tensor_tensor(l3.mv[:, 0:1], l3.negm[:], l3.negm[:], ALU.mult)
                    v.tensor_scalar(l3.aggr[:, 1:2], l3.aggr[:, 1:2], 1.0 / D, None, ALU.mult, op1=ALU.bypass)
                    v.tensor_tensor(l3.aggr[:, 1:2], l3.aggr[:, 1:2], l3.mv[:, 0:1], ALU.subtract)
                    v.tensor_scalar(l3.vareps[:], l3.aggr[:, 1:2], EPS, None, ALU.add,
                                    op1=ALU.bypass).then_inc(s3_v2s, 1)
                    v.wait_ge(s3_s2v, 2 * tc + 2)
                    if tc >= 2:
                        v.wait_ge(s3_tp, (tc - 1) * 8)
                    v.tensor_scalar(x3[tc % 2][:], aor[tc % 2][:], l3.negm[:], l3.rstd[:],
                                    ALU.add, op1=ALU.mult).then_inc(s3_x3, 1)

            @blk.scalar
            def _(sc):
                for tc in range(TCH):
                    for j in range(2):
                        sc.wait_ge(s3_v2s, 2 * tc + j + 1)
                        sc.activation(l3.rstd[:], l3.vareps[:], AF.Ln)
                        sc.activation(l3.rstd[:], l3.rstd[:], AF.Exp, scale=-0.5).then_inc(s3_s2v, 1)
                    for g in range(8):
                        gi = tc * 8 + g
                        pt = pt3[g % 2]
                        sc.wait_ge(s3_tp, gi + 1)
                        for j in range(4):
                            dc = g * 4 + j
                            ins = sc.copy(xln3T[:, (dc * 8 + tc) * 128:(dc * 8 + tc + 1) * 128],
                                          pt[:, j * 128:(j + 1) * 128])
                        ins.then_inc(s3_te, 1)

            @blk.tensor
            def _(t):
                for tc in range(TCH):
                    t.wait_ge(s3_x3, tc + 1)
                    for g in range(8):
                        gi = tc * 8 + g
                        pt = pt3[g % 2]
                        if gi >= 2:
                            t.wait_ge(s3_te, gi - 1)
                        for j in range(4):
                            dc = g * 4 + j
                            ins = t.transpose(pt[:, j * 128:(j + 1) * 128],
                                              x3[tc % 2][:, dc * 128:(dc + 1) * 128], ident[:])
                            if j == 3:
                                ins.then_inc(s3_tp, 1)
        pxg_st = ExitStack()
        pxg = [p.ps(f"s3pxg{i}", [32, 512], F32, pxg_st) for i in range(2)]
        s3_xg2 = p.sem("s3_xg2")
        with nc.Block() as blk:
            @blk.tensor
            def _(t):
                for th in range(2):
                    for dc in range(KC):
                        ins = t.matmul(pxg[th][0:32, :], oneh[:, 31 - dc:63 - dc],
                                       xln3T[:, (dc * 8 + th * 4) * 128:(dc * 8 + th * 4 + 4) * 128],
                                       start=(dc == 0), stop=(dc == KC - 1))
                    ins.then_inc(s3_xg2, 1)

            @blk.scalar
            def _(sc):
                for th in range(2):
                    sc.wait_ge(s3_xg2, th + 1)
                    sc.copy(xgT[0:32, th * 512:(th + 1) * 512], pxg[th][0:32, :])
        pxg_st.close()
        s3b.close()

    nc.leave_named_scope("s3", _sid3, False)
    _sid4, _ = nc.enter_named_scope("s4", False)
    if "s4" in stages and level >= 9:
        s3c = ExitStack()
        sf32 = p.sb("s3c_sf", [NCH, D], F32, s3c)
        sbf = p.sb("s3c_sb", [NCH, D], BF16, s3c)
        zn32 = p.sb("s3c_zn", [32, II], F32, s3c)
        z8 = p.sb("s3c_z8", [32, 1376 * 4], U8, s3c)
        z4u = p.sb("s3c_z4", [32, II], U8, s3c)
        zsb = p.sb("s3c_zs", [32, II], BF16, s3c)
        zn32d = p.sb("s3c_znd", [NCH, D], F32, s3c)
        z8d = p.sb("s3c_z8d", [NCH, 2048], U8, s3c)
        z4ud = p.sb("s3c_z4d", [NCH, D], U8, s3c)
        zsbd = p.sb("s3c_zsd", [NCH, D], BF16, s3c)
        s3c_ld = p.sem("s3c_ld"); s3c_dn = p.sem("s3c_dn")
        pwarm = p.ps("pwarm", [128, 128], F32, s3c)

        def prep_flat(src_ap, dst_d):
            with nc.Block() as blk:
                @blk.sync
                def _(sync):
                    sync.dma_start(out=sf32[:], in_=src_ap).then_inc(s3c_ld, 16)
                    sync.wait_ge(s3c_ld, p.cnt_s3c + 16)
                p.cnt_s3c += 16
            with nc.Block() as blk:
                @blk.tensor
                def _(t):
                    for i in range(400):
                        t.matmul(pwarm[:], ident[:], ident[:], start=(i == 0), stop=(i == 399))

                @blk.vector
                def _(v):
                    v.tensor_copy(sbf[:], sf32[:])
            with nc.Block() as blk:
                @blk.sync
                def _(sync):
                    sync.dma_start(out=dst_d.ap(), in_=sbf[:]).then_inc(s3c_ld, 16)
                    sync.wait_ge(s3c_ld, p.cnt_s3c + 16)
                p.cnt_s3c += 16

        def prep_zs(z_ap, s_nat_ap, dst_d, rows, nb):
            zz8 = z8 if rows == 32 else z8d
            zz4 = z4u if rows == 32 else z4ud
            zzn = zn32 if rows == 32 else zn32d
            zzs = zsb if rows == 32 else zsbd
            nel = nb * 2
            with nc.Block() as blk:
                @blk.sync
                def _(sync):
                    sync.dma_start(out=zz8[0:rows, 0:nb], in_=z_ap.bitcast(U8)).then_inc(s3c_ld, 16)
                    sync.dma_start(out=zzn[0:rows, 0:nel], in_=s_nat_ap).then_inc(s3c_ld, 16)
                    sync.wait_ge(s3c_ld, p.cnt_s3c + 32)
                p.cnt_s3c += 32
            with nc.Block() as blk:
                @blk.tensor
                def _(t):
                    for i in range(400):
                        t.matmul(pwarm[:], ident[:], ident[:], start=(i == 0), stop=(i == 399))

                @blk.vector
                def _(v):
                    v.tensor_scalar(zzn[0:rows, 0:nel], zzn[0:rows, 0:nel], -1.0, None,
                                    ALU.mult, op1=ALU.bypass)
                    zv = zz4[0:rows, 0:nel].rearrange("p (n two) -> p n two", two=2)
                    v.tensor_scalar(zv[:, :, 0], zz8[0:rows, 0:nb], 15, None,
                                    ALU.bitwise_and, op1=ALU.bypass)
                    v.tensor_scalar(zv[:, :, 1], zz8[0:rows, 0:nb], 4, None,
                                    ALU.logical_shift_right, op1=ALU.bypass)
                    v.tensor_tensor(zzs[0:rows, 0:nel], zz4[0:rows, 0:nel],
                                    zzn[0:rows, 0:nel], ALU.mult)
            with nc.Block() as blk:
                @blk.sync
                def _(sync):
                    sync.dma_start(out=dst_d.ap(), in_=zzs[0:rows, 0:nel]).then_inc(s3c_ld, 16)
                    sync.wait_ge(s3c_ld, p.cnt_s3c + 16)
                p.cnt_s3c += 16

        p.cnt_s3c = 0
        prep_flat(sg_flat, sgf_bf)
        prep_flat(su_flat, suf_bf)
        prep_flat(sd_nat, sdf_bf)
        prep_zs(zg, sg_nat, zs_g_d, 32, 5504)
        prep_zs(zu, su_nat, zs_u_d, 32, 5504)
        prep_zs(zd, sd_nat, zs_d_d, NCH, 2048)
        s3c.close()

        s4a = ExitStack()
        dpk8 = [p.sb(f"s4dp{i}", [128, 2048], U8, s4a) for i in range(2)]
        du8 = [p.sb(f"s4du{i}", [128, D], U8, s4a) for i in range(2)]
        srow = [p.sb(f"s4sr{i}", [1, D], BF16, s4a) for i in range(2)]
        wdst = [p.sb(f"s4wd{i}", [128, D], BF16, s4a) for i in range(2)]
        pbc = [p.ps(f"s4pb{i}", [128, 1024], F32, s4a) for i in range(4)]
        s4_ld = p.sem("s4_ld"); s4_un = p.sem("s4_un"); s4_bc = p.sem("s4_bc")
        s4_mu = p.sem("s4_mu"); s4_sp = p.sem("s4_sp")

        with nc.Block() as blk:
            @blk.sync
            def _(sync):
                for c in range(NCH):
                    if c >= 2:
                        sync.wait_ge(s4_un, 2 * (c - 1))
                    sync.dma_start(out=dpk8[c % 2][:],
                                   in_=dpk[c * 128:(c + 1) * 128, :].bitcast(U8)).then_inc(s4_ld, 16)
                    if c >= 2:
                        sync.wait_ge(s4_bc, 4 * (c - 1))
                    sync.dma_start(out=srow[c % 2][:],
                                   in_=sdf_bf.ap()[c:c + 1, :]).then_inc(s4_ld, 16)

            @blk.tensor
            def _(t):
                for c in range(NCH):
                    t.wait_ge(s4_ld, 32 * (c + 1))
                    for j in range(4):
                        b = (4 * c + j) % 4
                        if 4 * c + j >= 4:
                            t.wait_ge(s4_mu, 4 * c + j - 3)
                        for hh in range(2):
                            ins = t.matmul(pbc[b][:, hh * 512:(hh + 1) * 512],
                                           ones1[:],
                                           srow[c % 2][0:1, j * 1024 + hh * 512:j * 1024 + (hh + 1) * 512],
                                           start=True, stop=True)
                        ins.then_inc(s4_bc, 1)

            @blk.vector
            def _(v):
                for c in range(NCH):
                    v.wait_ge(s4_ld, 32 * c + 16)
                    dv = du8[c % 2][:].rearrange("p (n two) -> p n two", two=2)
                    v.tensor_scalar(dv[:, :, 0], dpk8[c % 2][:], 15, None,
                                    ALU.bitwise_and, op1=ALU.bypass)
                    v.tensor_scalar(dv[:, :, 1], dpk8[c % 2][:], 4, None,
                                    ALU.logical_shift_right, op1=ALU.bypass).then_inc(s4_un, 2)
                    for j in range(4):
                        v.wait_ge(s4_bc, 4 * c + j + 1)
                        if c >= 2:
                            v.wait_ge(s4_sp, 16 * (c - 1))
                        v.tensor_tensor(wdst[c % 2][:, j * 1024:(j + 1) * 1024],
                                        du8[c % 2][:, j * 1024:(j + 1) * 1024],
                                        pbc[j][:], ALU.mult).then_inc(s4_mu, 1)

            @blk.scalar
            def _(sc):
                for c in range(NCH):
                    sc.wait_ge(s4_mu, 4 * (c + 1))
                    sc.dma_start(out=wd_bf.ap()[c, :, :],
                                 in_=wdst[c % 2][:]).then_inc(s4_sp, 16)
                sc.wait_ge(s4_sp, 16 * NCH)
        s4a.close()

        zq = ExitStack()
        zgq = p.sb("s4zg", [32, 22 * 128], BF16, zq)
        zuq = p.sb("s4zu", [32, 22 * 128], BF16, zq)
        zdq = p.sb("s4zd", [32, D], BF16, zq)
        hgT = p.sb("s4hg", [32, T], BF16, zq)
        s4b = {n: p.sem("s4b_" + n) for n in
               ["zl", "pk", "un", "bc", "mu", "pe", "si", "hm", "hg", "hge",
                "wl", "pd", "de", "ol", "os"]}
        CNT = {n: 0 for n in s4b}

        def quarter(qi, c0, cn):
            hst = ExitStack()
            hsl = p.sb(f"s4h{qi}", [128, cn * 1024], BF16, hst)
            gst = ExitStack()
            dpg = [p.sb(f"s4gp{qi}0", [128, 2048], U8, gst)] * 2
            dpu = [p.sb(f"s4up{qi}0", [128, 2048], U8, gst)] * 2
            ug8 = [p.sb(f"s4g8{qi}{i}", [128, D], U8, gst) for i in range(2)]
            uu8 = [p.sb(f"s4u8{qi}{i}", [128, D], U8, gst) for i in range(2)]
            wtg = [p.sb(f"s4wg{qi}{i}", [128, D], BF16, gst) for i in range(2)]
            wtu = [p.sb(f"s4wu{qi}{i}", [128, D], BF16, gst) for i in range(2)]
            srg = [p.sb(f"s4sg{qi}0", [1, D], BF16, gst)] * 2
            sru = [p.sb(f"s4su{qi}0", [1, D], BF16, gst)] * 2
            hss = p.sb(f"s4hs{qi}", [128, 512], BF16, gst)
            gps = ExitStack()
            pg = [p.ps(f"s4pg{qi}{i}", [128, 512], F32, gps) for i in range(2)]
            pu = [p.ps(f"s4pu{qi}{i}", [128, 512], F32, gps) for i in range(2)]
            pb2 = [p.ps(f"s4pb{qi}{i}", [128, 1024], F32, gps) for i in range(2)]

            with nc.Block() as blk:
                @blk.sync
                def _(sync):
                    sync.dma_start(out=zgq[:, 0:cn * 128],
                                   in_=zs_g_d.ap()[:, c0 * 128:(c0 + cn) * 128]).then_inc(s4b["zl"], 16)
                    sync.dma_start(out=zuq[:, 0:cn * 128],
                                   in_=zs_u_d.ap()[:, c0 * 128:(c0 + cn) * 128]).then_inc(s4b["zl"], 16)
                    sync.dma_start(out=zdq[0:cn, :],
                                   in_=zs_d_d.ap()[c0:c0 + cn, :]).then_inc(s4b["zl"], 16)
                    CNT["zl"] += 48
                    for ci in range(cn):
                        c = c0 + ci
                        if ci >= 1:
                            sync.wait_ge(s4b["un"], CNT["un"] + 4 * ci)
                        sync.dma_start(out=dpg[ci % 2][:],
                                       in_=gpk[c, :, :].bitcast(U8)).then_inc(s4b["pk"], 16)
                        sync.dma_start(out=dpu[ci % 2][:],
                                       in_=upk[c, :, :].bitcast(U8)).then_inc(s4b["pk"], 16)
                        if ci >= 1:
                            sync.wait_ge(s4b["bc"], CNT["bc"] + 8 * ci)
                        sync.dma_start(out=srg[ci % 2][:],
                                       in_=sgf_bf.ap()[c:c + 1, :]).then_inc(s4b["pk"], 16)
                        sync.dma_start(out=sru[ci % 2][:],
                                       in_=suf_bf.ap()[c:c + 1, :]).then_inc(s4b["pk"], 16)

                @blk.tensor
                def _(t):
                    t.wait_ge(s4b["zl"], CNT["zl"])
                    for ci in range(cn):
                        t.wait_ge(s4b["pk"], CNT["pk"] + 64 * (ci + 1))
                        for m, sr in [(0, srg), (1, sru)]:
                            for j in range(4):
                                bi = 8 * ci + 4 * m + j
                                if bi >= 2:
                                    t.wait_ge(s4b["mu"], CNT["mu"] + bi - 1)
                                for hh in range(2):
                                    ins = t.matmul(pb2[bi % 2][:, hh * 512:(hh + 1) * 512],
                                                   ones1[:],
                                                   sr[ci % 2][0:1, j * 1024 + hh * 512:
                                                              j * 1024 + (hh + 1) * 512],
                                                   start=True, stop=True)
                                ins.then_inc(s4b["bc"], 1)
                        for m, wt, zz in [(0, wtg, zgq), (1, wtu, zuq)]:
                            t.wait_ge(s4b["mu"], CNT["mu"] + 8 * ci + 4 * (m + 1))
                            pp = pg if m == 0 else pu
                            for th in range(2):
                                if ci >= 1 and m == 0:
                                    t.wait_ge(s4b["si"], CNT["si"] + 2 * (ci - 1) + th + 1)
                                if ci >= 1 and m == 1:
                                    t.wait_ge(s4b["hm"], CNT["hm"] + 2 * (ci - 1) + th + 1)
                                for kc in range(KC):
                                    ins = t.matmul(pp[th][:],
                                                   wt[ci % 2][:, kc * 128:(kc + 1) * 128],
                                                   xln3T[:, (kc * 8 + th * 4) * 128:
                                                         (kc * 8 + th * 4 + 4) * 128],
                                                   start=(kc == 0), stop=False)
                                ins = t.matmul(pp[th][:], zz[:, ci * 128:(ci + 1) * 128],
                                               xgT[0:32, th * 512:(th + 1) * 512],
                                               start=False, stop=True)
                                ins.then_inc(s4b["pe"], 1)
                    phg = [pb2[0][0:32, 0:512], pb2[1][0:32, 0:512]]
                    t.wait_ge(s4b["hm"], CNT["hm"] + 2 * cn)
                    for ci in range(cn):
                        for th in range(2):
                            ins = t.matmul(phg[th][0:22, :], oneh[:, 31 - ci:53 - ci],
                                           hsl[:, ci * 1024 + th * 512:ci * 1024 + (th + 1) * 512],
                                           start=(ci == 0), stop=(ci == cn - 1))
                            if ci == cn - 1:
                                ins.then_inc(s4b["hg"], 1)

                @blk.vector
                def _(v):
                    for ci in range(cn):
                        v.wait_ge(s4b["pk"], CNT["pk"] + 64 * ci + 32)
                        for u8s, dps in [(ug8, dpg), (uu8, dpu)]:
                            uv = u8s[ci % 2][:].rearrange("p (n two) -> p n two", two=2)
                            v.tensor_scalar(uv[:, :, 0], dps[ci % 2][:], 15, None,
                                            ALU.bitwise_and, op1=ALU.bypass)
                            v.tensor_scalar(uv[:, :, 1], dps[ci % 2][:], 4, None,
                                            ALU.logical_shift_right, op1=ALU.bypass).then_inc(s4b["un"], 2)
                        for m in range(2):
                            u8s = ug8 if m == 0 else uu8
                            wt = wtg if m == 0 else wtu
                            for j in range(4):
                                bi = 8 * ci + 4 * m + j
                                v.wait_ge(s4b["bc"], CNT["bc"] + bi + 1)
                                if ci >= 2 and m == 0 and j == 0:
                                    v.wait_ge(s4b["pe"], CNT["pe"] + 4 * (ci - 1))
                                v.tensor_tensor(wt[ci % 2][:, j * 1024:(j + 1) * 1024],
                                                u8s[ci % 2][:, j * 1024:(j + 1) * 1024],
                                                pb2[bi % 2][:], ALU.mult).then_inc(s4b["mu"], 1)
                        for th in range(2):
                            v.wait_ge(s4b["pe"], CNT["pe"] + 4 * ci + 3 + th)
                            v.wait_ge(s4b["si"], CNT["si"] + 2 * ci + th + 1)
                            v.tensor_tensor(hsl[:, ci * 1024 + th * 512:ci * 1024 + (th + 1) * 512],
                                            pu[th][:], hsil[th][:], ALU.mult).then_inc(s4b["hm"], 1)

                @blk.scalar
                def _(sc):
                    for ci in range(cn):
                        for th in range(2):
                            sc.wait_ge(s4b["pe"], CNT["pe"] + 4 * ci + th + 1)
                            if ci >= 1:
                                sc.wait_ge(s4b["hm"], CNT["hm"] + 2 * (ci - 1) + th + 1)
                            sc.activation(hsil[th][:], pg[th][:], AF.Silu).then_inc(s4b["si"], 1)
                    sc.wait_ge(s4b["hg"], CNT["hg"] + 2)
                    sc.copy(hgT[0:cn, 0:512], pb2[0][0:cn, 0:512])
                    ins = sc.copy(hgT[0:cn, 512:1024], pb2[1][0:cn, 0:512])
                    ins.then_inc(s4b["hge"], 1)

            CNT["zl"] += 0
            CNT["pk"] += 64 * cn
            CNT["un"] += 4 * cn
            CNT["bc"] += 8 * cn
            CNT["mu"] += 8 * cn
            CNT["pe"] += 4 * cn
            CNT["si"] += 2 * cn
            CNT["hm"] += 2 * cn
            CNT["hg"] += 2
            CNT["hge"] += 1
            gst.close()
            gps.close()

            dst_ = ExitStack()
            wdsl = [p.sb(f"s4wl{qi}{i}", [128, cn * 512], BF16, dst_) for i in range(2)]
            odst = [p.sb(f"s4od{qi}{i}", [128, 512], BF16, dst_) for i in range(2)]
            opv = [p.sb(f"s4ov{qi}{i}", [128, 512], BF16, dst_) for i in range(2)]
            dps_ = ExitStack()
            pd = [p.ps(f"s4pd{qi}{i}", [128, 512], F32, dps_) for i in range(2)]

            with nc.Block() as blk:
                @blk.sync
                def _(sync):
                    for ob in range(TCH):
                        if ob >= 2:
                            sync.wait_ge(s4b["pd"], CNT["pd"] + 8 * (ob - 1))
                        for ci in range(cn):
                            sync.dma_start(out=wdsl[ob % 2][:, ci * 512:(ci + 1) * 512],
                                           in_=wd_bf.ap()[c0 + ci, :, ob * 512:(ob + 1) * 512]
                                           ).then_inc(s4b["wl"], 16)
                        if qi > 0:
                            for tc in range(TCH):
                                idx = ob * 8 + tc
                                if idx >= 2:
                                    sync.wait_ge(s4b["de"], CNT["de"] + idx - 1)
                                sync.dma_start(out=opv[idx % 2][:],
                                               in_=op_d.ap()[tc * 128:(tc + 1) * 128,
                                                             ob * 512:(ob + 1) * 512]
                                               ).then_inc(s4b["ol"], 16)

                @blk.tensor
                def _(t):
                    for ob in range(TCH):
                        t.wait_ge(s4b["wl"], CNT["wl"] + 16 * cn * (ob + 1))
                        for tc in range(TCH):
                            idx = ob * 8 + tc
                            if idx >= 2:
                                t.wait_ge(s4b["de"], CNT["de"] + idx - 1)
                            for ci in range(cn):
                                t.matmul(pd[idx % 2][:],
                                         hsl[:, ci * 1024 + tc * 128:ci * 1024 + tc * 128 + 128],
                                         wdsl[ob % 2][:, ci * 512:(ci + 1) * 512],
                                         start=(ci == 0), stop=False)
                            ins = t.matmul(pd[idx % 2][:], hgT[0:cn, tc * 128:(tc + 1) * 128],
                                           zdq[0:cn, ob * 512:(ob + 1) * 512],
                                           start=False, stop=True)
                            ins.then_inc(s4b["pd"], 1)

                @blk.vector
                def _(v):
                    for ob in range(TCH):
                        for tc in range(TCH):
                            idx = ob * 8 + tc
                            v.wait_ge(s4b["pd"], CNT["pd"] + idx + 1)
                            if idx >= 2:
                                v.wait_ge(s4b["os"], CNT["os"] + 16 * (idx - 1))
                            if qi == 0:
                                v.tensor_copy(odst[idx % 2][:],
                                              pd[idx % 2][:]).then_inc(s4b["de"], 1)
                            else:
                                v.wait_ge(s4b["ol"], CNT["ol"] + 16 * (idx + 1))
                                v.tensor_tensor(odst[idx % 2][:], pd[idx % 2][:],
                                                opv[idx % 2][:], ALU.add).then_inc(s4b["de"], 1)

                @blk.scalar
                def _(sc):
                    for ob in range(TCH):
                        for tc in range(TCH):
                            idx = ob * 8 + tc
                            sc.wait_ge(s4b["de"], CNT["de"] + idx + 1)
                            sc.dma_start(out=op_d.ap()[tc * 128:(tc + 1) * 128,
                                                       ob * 512:(ob + 1) * 512],
                                         in_=odst[idx % 2][:]).then_inc(s4b["os"], 16)
                    sc.wait_ge(s4b["os"], CNT["os"] + 16 * 64)

            CNT["wl"] += 16 * cn * TCH
            CNT["pd"] += 64
            CNT["de"] += 64
            CNT["os"] += 16 * 64
            if qi > 0:
                CNT["ol"] += 16 * 64
            dps_.close()
            dst_.close()
            hst.close()

        hsil = [p.sb(f"s4hsil{i}", [128, 512], BF16, zq) for i in range(2)]
        for qi, (c0, cn) in enumerate(QUARTERS):
            quarter(qi, c0, cn)
        zq.close()

    if "dump" in p.taps:
        tap_attno = nc.dram_tensor("tap_attno", [T, D], F32, kind="ExternalOutput").ap()
        tap_xp = nc.dram_tensor("tap_xp", [T, D], F32, kind="ExternalOutput").ap()
        tdmp = p.sem("tdmp")
        with nc.Block() as blk:
            @blk.sync
            def _(sync):
                sync.dma_start(out=tap_attno, in_=attno_d.ap()).then_inc(tdmp, 16)
                sync.dma_start(out=tap_xp, in_=xp_d.ap()).then_inc(tdmp, 16)
                sync.wait_ge(tdmp, 32)

    if "dump2" in p.taps:
        tap_op = nc.dram_tensor("tap_op", [T, D], BF16, kind="ExternalOutput").ap()
        tap_wd0 = nc.dram_tensor("tap_wd0", [128, D], BF16, kind="ExternalOutput").ap()
        tap_x3T = nc.dram_tensor("tap_x3T", [128, KC * T], BF16, kind="ExternalOutput").ap()
        tap_xgT = nc.dram_tensor("tap_xgT", [32, T], BF16, kind="ExternalOutput").ap()
        tdmp2 = p.sem("tdmp2")
        with nc.Block() as blk:
            @blk.sync
            def _(sync):
                sync.dma_start(out=tap_op, in_=op_d.ap()).then_inc(tdmp2, 16)
                sync.dma_start(out=tap_wd0, in_=wd_bf.ap()[0, :, :]).then_inc(tdmp2, 16)
                sync.wait_ge(tdmp2, 32)

            @blk.vector
            def _(v):
                pass

        with nc.Block() as blk:
            @blk.sync
            def _(sync):
                sync.dma_start(out=tap_x3T, in_=xln3T[:]).then_inc(tdmp2, 16)
                sync.dma_start(out=tap_xgT, in_=xgT[:]).then_inc(tdmp2, 16)
                sync.wait_ge(tdmp2, 64)

    nc.leave_named_scope("s4", _sid4, False)
    _sid5, _ = nc.enter_named_scope("s5", False)
    if "s5" in stages and level >= 10:
        s5 = ExitStack()
        opr = [p.sb(f"s5op{i}", [128, D], BF16, s5) for i in range(2)]
        xpr = [p.sb(f"s5xp{i}", [128, D], F32, s5) for i in range(2)]
        sq5 = p.sb("s5sq", [128, D], F32, s5)
        ln5 = p.sb("s5ln", [128, D], F32, s5)

        class L5:
            pass
        l5 = L5()
        l5.aggr = p.sb("s5ag", [128, 2], F32, s5)
        l5.negm = p.sb("s5nm", [128, 1], F32, s5)
        l5.vareps = p.sb("s5ve", [128, 1], F32, s5)
        l5.rstd = p.sb("s5rs", [128, 1], F32, s5)
        l5.mv = p.sb("s5mv", [128, 1], F32, s5)
        s5_ld = p.sem("s5_ld"); s5_v2s = p.sem("s5_v2s"); s5_s2v = p.sem("s5_s2v")
        s5_fin = p.sem("s5_fin"); s5_st = p.sem("s5_st")

        with nc.Block() as blk:
            @blk.sync
            def _(sync):
                for tc in range(TCH):
                    if tc >= 2:
                        sync.wait_ge(s5_fin, tc - 1)
                    sync.dma_start(out=opr[tc % 2][:],
                                   in_=op_d.ap()[tc * 128:(tc + 1) * 128, :]).then_inc(s5_ld, 16)
                    sync.dma_start(out=xpr[tc % 2][:],
                                   in_=xp_d.ap()[tc * 128:(tc + 1) * 128, :]).then_inc(s5_ld, 16)
                    sync.wait_ge(s5_fin, tc + 1)
                    sync.dma_start(out=out[tc * 128:(tc + 1) * 128, :],
                                   in_=xpr[tc % 2][:]).then_inc(s5_st, 16)
                sync.wait_ge(s5_st, 16 * TCH)

            @blk.vector
            def _(v):
                for tc in range(TCH):
                    v.wait_ge(s5_ld, 32 * (tc + 1))
                    v.tensor_reduce(l5.aggr[:, 0:1], opr[tc % 2][:], AX, ALU.add)
                    v.tensor_tensor(sq5[:], opr[tc % 2][:], opr[tc % 2][:], ALU.mult)
                    v.tensor_reduce(l5.aggr[:, 1:2], sq5[:], AX, ALU.add)
                    v.tensor_scalar(l5.negm[:], l5.aggr[:, 0:1], -1.0 / D, None, ALU.mult, op1=ALU.bypass)
                    v.tensor_tensor(l5.mv[:], l5.negm[:], l5.negm[:], ALU.mult)
                    v.tensor_scalar(l5.aggr[:, 1:2], l5.aggr[:, 1:2], 1.0 / D, None, ALU.mult, op1=ALU.bypass)
                    v.tensor_tensor(l5.aggr[:, 1:2], l5.aggr[:, 1:2], l5.mv[:], ALU.subtract)
                    v.tensor_scalar(l5.vareps[:], l5.aggr[:, 1:2], EPS, None, ALU.add,
                                    op1=ALU.bypass).then_inc(s5_v2s, 1)
                    v.wait_ge(s5_s2v, tc + 1)
                    v.tensor_scalar(ln5[:], opr[tc % 2][:], l5.negm[:], l5.rstd[:], ALU.add, op1=ALU.mult)
                    v.tensor_tensor(xpr[tc % 2][:], xpr[tc % 2][:], ln5[:],
                                    ALU.add).then_inc(s5_fin, 1)

            @blk.scalar
            def _(sc):
                for tc in range(TCH):
                    sc.wait_ge(s5_v2s, tc + 1)
                    sc.activation(l5.rstd[:], l5.vareps[:], AF.Ln)
                    sc.activation(l5.rstd[:], l5.rstd[:], AF.Exp, scale=-0.5).then_inc(s5_s2v, 1)
        s5.close()
    nc.leave_named_scope("s5", _sid5, False)

    return nc


def _prep(inputs):
    gq = np.ascontiguousarray(np.asarray(inputs["L0_ffn_gate_qweight"]))
    uq = np.ascontiguousarray(np.asarray(inputs["L0_ffn_up_qweight"]))
    dq = np.ascontiguousarray(np.asarray(inputs["L0_ffn_down_qweight"]))
    sg = np.ascontiguousarray(np.asarray(inputs["L0_ffn_gate_scales"]))
    su = np.ascontiguousarray(np.asarray(inputs["L0_ffn_up_scales"]))
    sd = np.ascontiguousarray(np.asarray(inputs["L0_ffn_down_scales"]))

    def retile(q):
        u8 = q.view(np.uint8).reshape(KC, 128, NCH, 64)
        return np.ascontiguousarray(u8.transpose(2, 1, 0, 3)).reshape(NCH, 128, 2048).view(np.int32)

    def sflat(s):
        return np.ascontiguousarray(s.reshape(KC, NCH, 128).transpose(1, 0, 2)).reshape(NCH, D)

    oneh = np.zeros((128, 64), dtype=ml_dtypes.bfloat16)
    oneh[:, 31] = 1.0
    shared = {
        "wqT": np.ascontiguousarray(np.asarray(inputs["L0_attn_w_q"]).T),
        "wkT": np.ascontiguousarray(np.asarray(inputs["L0_attn_w_k"]).T),
        "wvT": np.ascontiguousarray(np.asarray(inputs["L0_attn_w_v"]).T),
        "woT": np.ascontiguousarray(np.asarray(inputs["L0_attn_w_o"]).T),
        "gpk": retile(gq), "upk": retile(uq),
        "dpk": dq,
        "sg_flat": sflat(sg), "su_flat": sflat(su),
        "sg_nat": sg, "su_nat": su, "sd_nat": sd,
        "zg": np.ascontiguousarray(np.asarray(inputs["L0_ffn_gate_qzeros"])),
        "zu": np.ascontiguousarray(np.asarray(inputs["L0_ffn_up_qzeros"])),
        "zd": np.ascontiguousarray(np.asarray(inputs["L0_ffn_down_qzeros"])),
        "c_id": np.eye(128, dtype=ml_dtypes.bfloat16),
        "c_oneh": oneh,
        "c_ones1": np.ones((1, 128), dtype=ml_dtypes.bfloat16),
    }
    x = np.asarray(inputs["L0_attn_x"])
    in_maps = []
    for c in range(8):
        m = dict(shared)
        m["x_in"] = np.ascontiguousarray(x[c // 4, (c % 4) * T:(c % 4 + 1) * T, :])
        in_maps.append(m)
    return in_maps


def kernel(**inputs):
    in_maps = _prep(inputs)
    nc = build()
    res = run_bass_kernel_spmd(nc, in_maps, core_ids=list(range(8)))
    x = np.asarray(inputs["L0_attn_x"])
    outp = np.empty((2, S, D), dtype=np.float32)
    for c in range(8):
        outp[c // 4, (c % 4) * T:(c % 4 + 1) * T, :] = res.results[c]["out"]
    return outp


# version 09
# speedup vs baseline: 1.0164x; 1.0164x over previous
import numpy as np
import ml_dtypes
from contextlib import ExitStack

import concourse.bass as bass
import concourse.mybir as mybir
from concourse.bass_utils import run_bass_kernel_spmd

F32 = mybir.dt.float32
BF16 = mybir.dt.bfloat16
U8 = mybir.dt.uint8
I32 = mybir.dt.int32
AF = mybir.ActivationFunctionType
ALU = mybir.AluOpType
AX = mybir.AxisListType.X

D = 4096
T = 1024
S = 4096
H = 32
II = 11008
NCH = 86
KC = 32
TCH = 8
RSQRT_DK = 0.08838834764831845
EPS = 1e-5
QUARTERS = [(0, 22), (22, 22), (44, 21), (65, 21)]


class P:

    def __init__(self, nc, taps):
        self.nc = nc
        self.st = ExitStack()
        self.taps = set(taps)
        self._semnames = set()

    def sem(self, name):
        assert name not in self._semnames
        self._semnames.add(name)
        return self.st.enter_context(self.nc.semaphore(name))

    def sb(self, name, shape, dt, stack=None):
        return (stack or self.st).enter_context(self.nc.sbuf_tensor(name, shape, dt))

    def ps(self, name, shape, dt=F32, stack=None):
        return (stack or self.st).enter_context(self.nc.psum_tensor(name, shape, dt))


def _ln_tile(v, sc, src_ap, dst_ap, c, tc, sem_v2s, sem_s2v, sq_scratch=None):
    sq = dst_ap if sq_scratch is None else sq_scratch
    v.tensor_reduce(c.aggr[:, 0:1], src_ap, AX, ALU.add)
    v.tensor_tensor(sq, src_ap, src_ap, ALU.mult)
    v.tensor_reduce(c.aggr[:, 1:2], sq, AX, ALU.add)
    v.tensor_scalar(c.negm[:], c.aggr[:, 0:1], -1.0 / D, None, ALU.mult, op1=ALU.bypass)
    v.tensor_tensor(c.mv[:, 0:1], c.negm[:], c.negm[:], ALU.mult)
    v.tensor_scalar(c.aggr[:, 1:2], c.aggr[:, 1:2], 1.0 / D, None, ALU.mult, op1=ALU.bypass)
    v.tensor_tensor(c.aggr[:, 1:2], c.aggr[:, 1:2], c.mv[:, 0:1], ALU.subtract)
    v.tensor_scalar(c.vareps[:], c.aggr[:, 1:2], EPS, None, ALU.add,
                    op1=ALU.bypass).then_inc(sem_v2s, 1)
    v.wait_ge(sem_s2v, tc + 1)
    v.tensor_scalar(dst_ap, src_ap, c.negm[:], c.rstd[:], ALU.add, op1=ALU.mult)


def _ln_scalar(sc, c, tc, sem_v2s, sem_s2v):
    sc.wait_ge(sem_v2s, tc + 1)
    sc.activation(c.rstd[:], c.vareps[:], AF.Ln)
    sc.activation(c.rstd[:], c.rstd[:], AF.Exp, scale=-0.5).then_inc(sem_s2v, 1)


def build(taps=(), stages=("s1", "s2", "s3", "s4", "s5"), level=99, skip_cc=False, dbg_unnorm=False):
    nc = bass.Bass("TRN2", target_bir_lowering=False, detect_race_conditions=False)
    p = P(nc, taps)
    stages = set(stages)

    x_in = nc.dram_tensor("x_in", [T, D], F32, kind="ExternalInput").ap()
    wqT = nc.dram_tensor("wqT", [D, D], F32, kind="ExternalInput").ap()
    wkT = nc.dram_tensor("wkT", [D, D], F32, kind="ExternalInput").ap()
    wvT = nc.dram_tensor("wvT", [D, D], F32, kind="ExternalInput").ap()
    woT = nc.dram_tensor("woT", [D, D], F32, kind="ExternalInput").ap()
    gpk = nc.dram_tensor("gpk", [NCH, 128, 512], I32, kind="ExternalInput").ap()
    upk = nc.dram_tensor("upk", [NCH, 128, 512], I32, kind="ExternalInput").ap()
    dpk = nc.dram_tensor("dpk", [II, 512], I32, kind="ExternalInput").ap()
    sg_flat = nc.dram_tensor("sg_flat", [NCH, D], F32, kind="ExternalInput").ap()
    su_flat = nc.dram_tensor("su_flat", [NCH, D], F32, kind="ExternalInput").ap()
    sg_nat = nc.dram_tensor("sg_nat", [32, II], F32, kind="ExternalInput").ap()
    su_nat = nc.dram_tensor("su_nat", [32, II], F32, kind="ExternalInput").ap()
    sd_nat = nc.dram_tensor("sd_nat", [NCH, D], F32, kind="ExternalInput").ap()
    zg = nc.dram_tensor("zg", [32, 1376], I32, kind="ExternalInput").ap()
    zu = nc.dram_tensor("zu", [32, 1376], I32, kind="ExternalInput").ap()
    zd = nc.dram_tensor("zd", [NCH, 512], I32, kind="ExternalInput").ap()
    c_id = nc.dram_tensor("c_id", [128, 128], BF16, kind="ExternalInput").ap()
    c_oneh = nc.dram_tensor("c_oneh", [128, 64], BF16, kind="ExternalInput").ap()
    c_ones1 = nc.dram_tensor("c_ones1", [1, 128], BF16, kind="ExternalInput").ap()

    out = nc.dram_tensor("out", [T, D], F32, kind="ExternalOutput").ap()

    kT_loc = nc.dram_tensor("kT_loc", [D, T], BF16)
    v_loc = nc.dram_tensor("v_loc", [T, D], BF16)
    knm_loc = nc.dram_tensor("knm_loc", [32, 1], F32)
    kT_g = nc.dram_tensor("kT_g", [4 * D, T], BF16)
    v_g = nc.dram_tensor("v_g", [S, D], BF16)
    knm_g = nc.dram_tensor("knm_g", [128, 1], F32)
    qT_d = nc.dram_tensor("qT_d", [D, T], BF16)
    negb_d = nc.dram_tensor("negb_d", [32, 1], F32)
    attnT_d = nc.dram_tensor("attnT_d", [D, T], BF16)
    attno_d = nc.dram_tensor("attno_d", [T, D], F32)
    xp_d = nc.dram_tensor("xp_d", [T, D], F32)
    sgf_bf = nc.dram_tensor("sgf_bf", [NCH, D], BF16)
    suf_bf = nc.dram_tensor("suf_bf", [NCH, D], BF16)
    sdf_bf = nc.dram_tensor("sdf_bf", [NCH, D], BF16)
    zs_g_d = nc.dram_tensor("zs_g_d", [32, II], BF16)
    zs_u_d = nc.dram_tensor("zs_u_d", [32, II], BF16)
    zs_d_d = nc.dram_tensor("zs_d_d", [NCH, D], BF16)
    wd_bf = nc.dram_tensor("wd_bf", [NCH, 128, D], BF16)
    op_d = nc.dram_tensor("op_d", [T, D], BF16)

    def tap(name, shape, dt):
        if name in p.taps:
            return nc.dram_tensor("tap_" + name, shape, dt, kind="ExternalOutput").ap()
        return None

    tap_xln1 = tap("xln1", [T, D], BF16)
    tap_negb = tap("negb", [32, T], BF16)
    tap_probs = tap("probs", [S, 512], BF16)
    tap_kTg = tap("kTg", [4 * D, T], BF16)
    tap_vloc = tap("vloc", [T, D], BF16)
    tap_vg = tap("vg", [S, D], BF16)
    tap_qT = tap("qT", [D, T], BF16)

    ident = p.sb("ident", [128, 128], BF16)
    oneh = p.sb("oneh", [128, 64], BF16)
    ones1 = p.sb("ones1", [1, 128], BF16)
    negb = p.sb("negb", [32, T], BF16)
    xgT = p.sb("xgT", [32, T], BF16)
    knmx = p.sb("knmx", [32, 1], F32)
    xln3T = p.sb("xln3T", [128, KC * T], BF16)

    cc_sem = p.sem("cc")
    CC = {"n": 0}

    cst = p.sem("cst")
    with nc.Block() as blk:
        @blk.sync
        def _(sync):
            sync.dma_start(out=ident[:], in_=c_id).then_inc(cst, 16)
            sync.dma_start(out=oneh[:], in_=c_oneh).then_inc(cst, 16)
            sync.dma_start(out=ones1[:], in_=c_ones1).then_inc(cst, 16)
            sync.wait_ge(cst, 48)

    _sid1, _ = nc.enter_named_scope("s1", False)
    if "s1" in stages:
        s1 = ExitStack()
        xln1T = p.sb("xln1T", [128, KC * T], BF16, s1)

        class LNC:
            pass
        lc = LNC()
        lc.mv = p.sb("ln_mv", [128, 48], F32, s1)
        lc.aggr = p.sb("ln_ag", [128, 2], F32, s1)
        lc.negm = p.sb("ln_nm", [128, 1], F32, s1)
        lc.vareps = p.sb("ln_ve", [128, 1], F32, s1)
        lc.rstd = p.sb("ln_rs", [128, 1], F32, s1)
        lnq = p.sb("lnq", [32, T], F32, s1)
        halflnk = p.sb("halflnk", [32, 1], F32, s1)
        s1a = ExitStack()
        xbuf = [p.sb(f"xbuf{i}", [128, D], F32, s1a) for i in range(2)]
        xls = [p.sb(f"xls{i}", [128, D], BF16, s1a) for i in range(2)]
        ptA = p.ps("ptA", [128, 512], BF16, s1)
        ptB = p.ps("ptB", [128, 512], BF16, s1)

        x_ld = p.sem("x_ld"); x_fr = p.sem("x_fr"); xn = p.sem("xn")
        tapx = p.sem("tapx") if tap_xln1 is not None else None
        v2s = p.sem("s1_v2s"); s2v = p.sem("s1_s2v")
        pe_t = p.sem("pe_t"); ev2 = p.sem("ev2")

        with nc.Block() as blk:
            @blk.sync
            def _(sync):
                for tc in range(TCH):
                    if tap_xln1 is not None and tc >= 2:
                        tp = tc - 2
                        sync.wait_ge(xn, tp + 1)
                        sync.dma_start(out=tap_xln1[tp * 128:(tp + 1) * 128, :],
                                       in_=xls[tp % 2][:]).then_inc(tapx, 16)
                    if tc >= 2:
                        sync.wait_ge(x_fr, tc - 1)
                    sync.dma_start(out=xbuf[tc % 2][:],
                                   in_=x_in[tc * 128:(tc + 1) * 128, :]).then_inc(x_ld, 16)
                if tap_xln1 is not None:
                    for tp in range(TCH - 2, TCH):
                        sync.wait_ge(xn, tp + 1)
                        sync.dma_start(out=tap_xln1[tp * 128:(tp + 1) * 128, :],
                                       in_=xls[tp % 2][:]).then_inc(tapx, 16)

            @blk.vector
            def _(v):
                for tc in range(TCH):
                    v.wait_ge(x_ld, 16 * (tc + 1))
                    if tc >= 2:
                        v.wait_ge(pe_t, (tc - 1) * 8)
                        if tap_xln1 is not None:
                            v.wait_ge(tapx, 16 * (tc - 1))
                    _ln_tile(v, None, xbuf[tc % 2][:], xls[tc % 2][:], lc, tc, v2s, s2v)
                    v.engine_nop().then_inc(x_fr, 1)
                    v.engine_nop().then_inc(xn, 1)

            @blk.scalar
            def _(sc):
                for tc in range(TCH):
                    _ln_scalar(sc, lc, tc, v2s, s2v)
                    for g in range(8):
                        gi = tc * 8 + g
                        pt = ptA if g % 2 == 0 else ptB
                        sc.wait_ge(pe_t, gi + 1)
                        for j in range(4):
                            dc = g * 4 + j
                            ins = sc.copy(xln1T[:, (dc * 8 + tc) * 128:(dc * 8 + tc + 1) * 128],
                                          pt[:, j * 128:(j + 1) * 128])
                        ins.then_inc(ev2, 1)

            @blk.tensor
            def _(t):
                for tc in range(TCH):
                    t.wait_ge(xn, tc + 1)
                    for g in range(8):
                        gi = tc * 8 + g
                        pt = ptA if g % 2 == 0 else ptB
                        if gi >= 2:
                            t.wait_ge(ev2, gi - 1)
                        for j in range(4):
                            dc = g * 4 + j
                            ins = t.transpose(pt[:, j * 128:(j + 1) * 128],
                                              xls[tc % 2][:, dc * 128:(dc + 1) * 128], ident[:])
                            if j == 3:
                                ins.then_inc(pe_t, 1)

        s1a.close()
        if level < 2:
            return nc
        s1b = ExitStack()
        wslab = [p.sb(f"wslab{i}", [128, KC * 128], BF16, s1b) for i in range(2)]
        kst = [p.sb(f"kst{i}", [128, T], BF16, s1b) for i in range(2)]
        sqst = [p.sb(f"sqst{i}", [128, 512], BF16, s1b) for i in range(2)]
        pk = [p.ps(f"pk{i}", [128, 512], F32, s1) for i in range(2)]
        pkn = p.ps("pkn", [32, T], F32, s1)
        pqn = p.ps("pqn", [32, T], F32, s1)

        def proj_feature_major(tag, w_src, out_dram, scale, pnorm):
            w_ld = p.sem(tag + "_wld"); w_fr = p.sem(tag + "_wfr")
            pe_m = p.sem(tag + "_pe"); ev_m = p.sem(tag + "_ev")
            sq_m = p.sem(tag + "_sq"); nrm = p.sem(tag + "_nrm"); sp_m = p.sem(tag + "_sp")

            with nc.Block() as blk:
                @blk.gpsimd
                def _(g):
                    for ec in range(KC):
                        if ec >= 2:
                            g.wait_ge(w_fr, ec - 1)
                        g.dma_start(
                            out=wslab[ec % 2][:].rearrange("p (dc e) -> p dc e", e=128),
                            in_=w_src[:, ec * 128:(ec + 1) * 128]
                            .rearrange("(dc p) e -> p dc e", p=128)).then_inc(w_ld, 16)

                @blk.tensor
                def _(t):
                    for ec in range(KC):
                        t.wait_ge(w_ld, 16 * (ec + 1))
                        for th in range(2):
                            idx = ec * 2 + th
                            if idx >= 2:
                                t.wait_ge(ev_m, idx - 1)
                            for dc in range(KC):
                                ins = t.matmul(
                                    pk[idx % 2][:],
                                    wslab[ec % 2][:, dc * 128:(dc + 1) * 128],
                                    xln1T[:, (dc * 8 + th * 4) * 128:(dc * 8 + th * 4 + 4) * 128],
                                    start=(dc == 0), stop=(dc == KC - 1))
                            ins.then_inc(pe_m, 1)
                            if th == 1:
                                last_slab_use = ins
                            if idx >= 1:
                                pidx = idx - 1
                                pec, pth = pidx // 2, pidx % 2
                                t.wait_ge(sq_m, pidx + 1)
                                t.matmul(pnorm[:, pth * 512:(pth + 1) * 512],
                                         oneh[:, 31 - pec:63 - pec], sqst[pidx % 2][:],
                                         start=(pec == 0), stop=(pec == KC - 1)).then_inc(nrm, 1)
                        t.wait_ge(pe_m, 2 * ec + 2)
                        t.nop().then_inc(w_fr, 1)
                    pidx = 2 * KC - 1
                    pec, pth = pidx // 2, pidx % 2
                    t.wait_ge(sq_m, pidx + 1)
                    t.matmul(pnorm[:, pth * 512:(pth + 1) * 512],
                             oneh[:, 31 - pec:63 - pec], sqst[pidx % 2][:],
                             start=(pec == 0), stop=(pec == KC - 1)).then_inc(nrm, 1)

                @blk.vector
                def _(v):
                    for idx in range(2 * KC):
                        ec, th = idx // 2, idx % 2
                        v.wait_ge(pe_m, idx + 1)
                        if th == 0 and ec >= 2:
                            v.wait_ge(sp_m, 16 * (ec - 1))
                        v.tensor_scalar(kst[ec % 2][:, th * 512:(th + 1) * 512],
                                        pk[idx % 2][:], scale, None, ALU.mult, op1=ALU.bypass)
                        v.engine_nop().then_inc(ev_m, 1)
                        if idx >= 2:
                            v.wait_ge(nrm, idx - 1)
                        v.tensor_tensor(sqst[idx % 2][:],
                                        kst[ec % 2][:, th * 512:(th + 1) * 512],
                                        kst[ec % 2][:, th * 512:(th + 1) * 512],
                                        ALU.mult).then_inc(sq_m, 1)

                @blk.sync
                def _(sync):
                    for ec in range(KC):
                        sync.wait_ge(sq_m, 2 * ec + 2)
                        sync.dma_start(out=out_dram[ec * 128:(ec + 1) * 128, :],
                                       in_=kst[ec % 2][:]).then_inc(sp_m, 16)
                    sync.wait_ge(sp_m, 16 * KC)
            return nrm, sp_m

        knrm, ksp = proj_feature_major("K", wkT, kT_loc.ap(), 1.0, pkn)
        if level < 3:
            return nc
        knm_dma = p.sem("knm_dma")
        knm_r = p.sem("knm_r")
        with nc.Block() as blk:
            @blk.vector
            def _(v):
                v.tensor_reduce(knmx[:], pkn[:], AX, ALU.max).then_inc(knm_r, 1)

            @blk.sync
            def _(sync):
                sync.wait_ge(knm_r, 1)
                sync.dma_start(out=knm_loc[:, :], in_=knmx[:]).then_inc(knm_dma, 16)

        if level < 4:
            return nc
        qnrm, qsp = proj_feature_major("Q", wqT, qT_d.ap(), RSQRT_DK, pqn)
        if level < 5:
            return nc

        s1b.close()
        vslab = [p.sb(f"vslab{i}", [128, KC * 512], BF16, s1) for i in range(2)]
        vst = [p.sb(f"vst{i}", [128, 512], BF16, s1) for i in range(2)]
        v_ld = p.sem("v_ld"); v_fr = p.sem("v_fr"); pe_v = p.sem("pe_v")
        vev = p.sem("vev"); vsp = p.sem("vsp")

        with nc.Block() as blk:
            @blk.gpsimd
            def _(g):
                for eh in range(TCH):
                    if eh >= 2:
                        g.wait_ge(v_fr, eh - 1)
                    g.dma_start(out=vslab[eh % 2][:].rearrange("p (dc e) -> p dc e", e=512),
                                in_=wvT[:, eh * 512:(eh + 1) * 512]
                                .rearrange("(dc p) e -> p dc e", p=128)).then_inc(v_ld, 16)
                g.wait_ge(vsp, 16 * 64)

            @blk.tensor
            def _(t):
                for eh in range(TCH):
                    t.wait_ge(v_ld, 16 * (eh + 1))
                    for tc in range(TCH):
                        idx = eh * 8 + tc
                        if idx >= 2:
                            t.wait_ge(vev, idx - 1)
                        for dc in range(KC):
                            ins = t.matmul(pk[idx % 2][:],
                                           xln1T[:, (dc * 8 + tc) * 128:(dc * 8 + tc + 1) * 128],
                                           vslab[eh % 2][:, dc * 512:(dc + 1) * 512],
                                           start=(dc == 0), stop=(dc == KC - 1))
                        ins.then_inc(pe_v, 1)
                    t.wait_ge(pe_v, 8 * (eh + 1))
                    t.nop().then_inc(v_fr, 1)

            @blk.vector
            def _(v):
                for idx in range(64):
                    v.wait_ge(pe_v, idx + 1)
                    if idx >= 2:
                        v.wait_ge(vsp, 16 * (idx - 1))
                    v.tensor_copy(vst[idx % 2][:], pk[idx % 2][:])
                    v.engine_nop().then_inc(vev, 1)

            @blk.sync
            def _(sync):
                for idx in range(64):
                    eh, tc = idx // 8, idx % 8
                    sync.wait_ge(vev, idx + 1)
                    sync.dma_start(out=v_loc.ap()[tc * 128:(tc + 1) * 128,
                                                  eh * 512:(eh + 1) * 512],
                                   in_=vst[idx % 2][:]).then_inc(vsp, 16)
                sync.wait_ge(vsp, 16 * 64)

        if not skip_cc:
            with nc.Block() as blk:
                @blk.tensor
                def _(t):
                    for i in range(2000):
                        t.matmul(pk[0][:, 0:128], ident[:], ident[:],
                                 start=(i == 0), stop=(i == 1999))

                @blk.gpsimd
                def _(g):
                    g.wait_ge(knm_dma, 16)
                    g.collective_compute("AllGather", ALU.bypass,
                                         replica_groups=[[0, 1, 2, 3], [4, 5, 6, 7]],
                                         ins=[kT_loc.ap().opt()],
                                         outs=[kT_g.ap().opt()]).then_inc(cc_sem, 1)
                    g.collective_compute("AllGather", ALU.bypass,
                                         replica_groups=[[0, 1, 2, 3], [4, 5, 6, 7]],
                                         ins=[knm_loc.ap().opt()],
                                         outs=[knm_g.ap().opt()]).then_inc(cc_sem, 1)
                    g.collective_compute("AllGather", ALU.bypass,
                                         replica_groups=[[0, 1, 2, 3], [4, 5, 6, 7]],
                                         ins=[v_loc.ap().opt()],
                                         outs=[v_g.ap().opt()]).then_inc(cc_sem, 1)
                    g.wait_ge(cc_sem, 3)
                    CC["n"] += 3

        if tap_vloc is not None:
            tvl = p.sem("tvl")
            with nc.Block() as blk:
                @blk.sync
                def _(sync):
                    sync.dma_start(out=tap_vloc, in_=v_loc.ap()).then_inc(tvl, 16)
                    sync.wait_ge(tvl, 16)

        if level < 6:
            return nc
        qsqm = p.sb("qsqm", [32, 1], F32, s1)
        bprep = p.sem("bprep")
        with nc.Block() as blk:
            @blk.sync
            def _(sync):
                sync.wait_ge(cc_sem, 2)
                sync.dma_start(out=knmx[:], in_=knm_g.ap()[0:32, :]).then_inc(bprep, 16)
                sync.dma_start(out=lc.mv[0:32, 0:1], in_=knm_g.ap()[32:64, :]).then_inc(bprep, 16)
                sync.dma_start(out=lc.mv[0:32, 1:2], in_=knm_g.ap()[64:96, :]).then_inc(bprep, 16)
                sync.dma_start(out=lc.mv[0:32, 2:3], in_=knm_g.ap()[96:128, :]).then_inc(bprep, 16)
                sync.wait_ge(bprep, 64)

        with nc.Block() as blk:
            @blk.vector
            def _(v):
                v.tensor_tensor(knmx[:], knmx[:], lc.mv[0:32, 0:1], ALU.max)
                v.tensor_tensor(knmx[:], knmx[:], lc.mv[0:32, 1:2], ALU.max)
                v.tensor_tensor(knmx[:], knmx[:], lc.mv[0:32, 2:3], ALU.max)

        with nc.Block() as blk:
            @blk.vector
            def _(v):
                v.tensor_reduce(qsqm[:], pqn[:], AX, ALU.max)

        with nc.Block() as blk:
            @blk.scalar
            def _(sc):
                sc.activation(halflnk[:], knmx[:], AF.Ln)
                sc.activation(lnq[0:32, 0:1], qsqm[:], AF.Ln)

        with nc.Block() as blk:
            @blk.vector
            def _(v):
                v.tensor_scalar(halflnk[:], halflnk[:], 0.5, None, ALU.mult, op1=ALU.bypass)

        with nc.Block() as blk:
            @blk.scalar
            def _(sc):
                sc.activation(lnq[0:32, 0:1], lnq[0:32, 0:1], AF.Exp, scale=0.5, bias=halflnk[:])

        nbspill = p.sem("nbspill")
        with nc.Block() as blk:
            @blk.vector
            def _(v):
                v.tensor_scalar(qsqm[:], lnq[0:32, 0:1], -1.0, -0.125, ALU.mult, op1=ALU.add)

        with nc.Block() as blk:
            @blk.sync
            def _(sync):
                sync.dma_start(out=negb_d.ap(), in_=qsqm[:]).then_inc(nbspill, 16)
                sync.wait_ge(nbspill, 16)

        if tap_kTg is not None or tap_vg is not None or tap_qT is not None:
            tkg = p.sem("tkg")
            with nc.Block() as blk:
                @blk.sync
                def _(sync):
                    sync.wait_ge(cc_sem, 3)
                    n = 0
                    for t_, s_ in [(tap_kTg, kT_g), (tap_vg, v_g), (tap_qT, qT_d)]:
                        if t_ is not None:
                            sync.dma_start(out=t_, in_=s_.ap()).then_inc(tkg, 16)
                            n += 16
                    sync.wait_ge(tkg, n)

        if tap_negb is not None:
            tapnb = p.sem("tapnb")
            with nc.Block() as blk:
                @blk.sync
                def _(sync):
                    sync.dma_start(out=tap_negb, in_=negb[:]).then_inc(tapnb, 16)
                    sync.wait_ge(tapnb, 16)

        s1.close()

    nc.leave_named_scope("s1", _sid1, False)
    _sid2, _ = nc.enter_named_scope("s2", False)
    if "s2" in stages and level >= 7:
        s2 = ExitStack()
        kT_t = [p.sb(f"s2kt{i}", [128, S], BF16, s2) for i in range(2)]
        v_t = [p.sb(f"s2vt{i}", [128, KC * 144], BF16, s2) for i in range(2)]
        q_t = [p.sb(f"s2qt{i}", [128, T], BF16, s2) for i in range(2)]
        nbc = [p.sb(f"s2nbc{i}", [128, 1], F32, s2) for i in range(2)]
        slab = [p.sb(f"s2sl{i}", [128, KC * 512], BF16, s2) for i in range(2)]
        ao_st = [p.sb(f"s2ao{i}", [128, 128], BF16, s2) for i in range(2)]
        at_st = [p.sb(f"s2at{i}", [128, T], BF16, s2) for i in range(2)]
        psc = [p.ps(f"s2psc{i}", [128, 1024], F32, s2) for i in range(2)]
        pav = [p.ps(f"s2pav{i}", [128, 132], F32, s2) for i in range(2)]
        ptr = [p.ps(f"s2ptr{i}", [128, 512], BF16, s2) for i in range(2)]
        rcp2 = [p.sb(f"s2rcp{i}", [128, 1], F32, s2) for i in range(2)]
        lnd = p.sb("s2lnd", [128, 1], F32, s2)

        s2_pf = p.sem("s2_pf"); s2_sc = p.sem("s2_sc"); s2_exp = p.sem("s2_exp")
        s2_avp = p.sem("s2_avp"); s2_ave = p.sem("s2_ave"); s2_tr = p.sem("s2_tr")
        s2_tre = p.sem("s2_tre"); s2_asp = p.sem("s2_asp"); s2_rc = p.sem("s2_rc")

        NU = 2 * H

        with nc.Block() as blk:
            @blk.gpsimd
            def _(g):
                g.memset(v_t[0][:].rearrange("p (kc w) -> p kc w", w=144)[:, :, 128:129], 1.0)
                g.memset(v_t[1][:].rearrange("p (kc w) -> p kc w", w=144)[:, :, 128:129], 1.0)

        def pe_scores(t, u):
            h, qh = u // 2, u % 2
            t.wait_ge(s2_pf, 112 * (h + 1))
            for kc in range(KC):
                g2 = u * 16 + kc // 2
                if kc % 2 == 0 and g2 >= 2:
                    t.wait_ge(s2_exp, g2 - 1)
                pb = psc[g2 % 2][:, (kc % 2) * 512:(kc % 2 + 1) * 512]
                t.matmul(pb, kT_t[h % 2][:, kc * 128:(kc + 1) * 128],
                         q_t[h % 2][:, qh * 512:(qh + 1) * 512],
                         start=True, stop=True).then_inc(s2_sc, 1)

        def pe_avtrans(t, u):
            h, qh = u // 2, u % 2
            t.wait_ge(s2_exp, 16 * (u + 1))
            for qc in range(4):
                a = 4 * u + qc
                if a >= 2:
                    t.wait_ge(s2_ave, a - 1)
                for kc in range(KC):
                    ins = t.matmul(pav[qc % 2][:, 0:129],
                                   slab[u % 2][:, kc * 512 + qc * 128:kc * 512 + (qc + 1) * 128],
                                   v_t[h % 2][:, kc * 144:kc * 144 + 129],
                                   start=(kc == 0), stop=(kc == KC - 1))
                ins.then_inc(s2_avp, 1)
                if a >= 1:
                    pa = a - 1
                    pu, pqc = pa // 4, pa % 4
                    t.wait_ge(s2_ave, pa + 1)
                    if pu >= 2 and pqc == 0:
                        t.wait_ge(s2_tre, pu - 1)
                    t.transpose(ptr[pu % 2][:, pqc * 128:(pqc + 1) * 128],
                                ao_st[pa % 2][:], ident[:]).then_inc(s2_tr, 1)

        with nc.Block() as blk:
            @blk.sync
            def _(sync):
                for h in range(H):
                    if h >= 2:
                        sync.wait_ge(s2_asp, 16 * (h - 1))
                    for r in range(4):
                        sync.dma_start(
                            out=kT_t[h % 2][:, r * T:(r + 1) * T],
                            in_=kT_g.ap()[r * D + h * 128:r * D + (h + 1) * 128, :]
                        ).then_inc(s2_pf, 16)
                    sync.dma_start(
                        out=v_t[h % 2][:].rearrange("p (kc w) -> p kc w", w=144)[:, :, 0:128],
                        in_=v_g.ap()[:, h * 128:(h + 1) * 128]
                        .rearrange("(kc p) e -> p kc e", p=128)).then_inc(s2_pf, 16)
                    sync.dma_start(out=q_t[h % 2][:],
                                   in_=qT_d.ap()[h * 128:(h + 1) * 128, :]).then_inc(s2_pf, 16)
                    sync.dma_start(out=nbc[h % 2][:],
                                   in_=negb_d.ap()[h:h + 1, :].to_broadcast((128, 1))).then_inc(s2_pf, 16)
                    if h >= 1:
                        ph = h - 1
                        sync.wait_ge(s2_tre, 2 * (ph + 1))
                        sync.dma_start(out=attnT_d.ap()[ph * 128:(ph + 1) * 128, :],
                                       in_=at_st[ph % 2][:]).then_inc(s2_asp, 16)
                ph = H - 1
                sync.wait_ge(s2_tre, 2 * (ph + 1))
                sync.dma_start(out=attnT_d.ap()[ph * 128:(ph + 1) * 128, :],
                               in_=at_st[ph % 2][:]).then_inc(s2_asp, 16)
                sync.wait_ge(s2_asp, 16 * H)

            @blk.tensor
            def _(t):
                for u in range(NU):
                    pe_scores(t, u)
                    if u >= 1:
                        pe_avtrans(t, u - 1)
                pe_avtrans(t, NU - 1)
                a = 4 * NU - 1
                t.wait_ge(s2_ave, a + 1)
                t.transpose(ptr[(a // 4) % 2][:, (a % 4) * 128:(a % 4 + 1) * 128],
                            ao_st[a % 2][:], ident[:]).then_inc(s2_tr, 1)

            def rcp_pair(sc, a):
                sc.wait_ge(s2_avp, a + 1)
                if a >= 2:
                    sc.wait_ge(s2_ave, a - 1)
                sc.activation(lnd[:], pav[(a % 4) % 2][:, 128:129], AF.Ln)
                sc.activation(rcp2[a % 2][:], lnd[:], AF.Exp,
                              scale=-1.0).then_inc(s2_rc, 1)

            @blk.scalar
            def _(sc):
                for u in range(NU):
                    h = u // 2
                    if u >= 2:
                        sc.wait_ge(s2_ave, 4 * (u - 1))
                    for t2 in range(16):
                        g2 = u * 16 + t2
                        sc.wait_ge(s2_sc, u * 32 + 2 * t2 + 2)
                        sc.activation(slab[u % 2][:, t2 * 1024:(t2 + 1) * 1024],
                                      psc[g2 % 2][:], AF.Exp,
                                      bias=nbc[h % 2][:]).then_inc(s2_exp, 1)
                    if u >= 1:
                        for qc in range(4):
                            rcp_pair(sc, 4 * (u - 1) + qc)
                for qc in range(4):
                    rcp_pair(sc, 4 * (NU - 1) + qc)

            @blk.vector
            def _(v):
                for u in range(NU):
                    h, qh = u // 2, u % 2
                    for qc in range(4):
                        a = 4 * u + qc
                        v.wait_ge(s2_avp, a + 1)
                        if a >= 2:
                            v.wait_ge(s2_tr, a - 1)
                        if dbg_unnorm == "den":
                            v.tensor_copy(ao_st[a % 2][:],
                                          pav[qc % 2][:, 1:129]).then_inc(s2_ave, 1)
                        elif dbg_unnorm:
                            v.tensor_copy(ao_st[a % 2][:],
                                          pav[qc % 2][:, 0:128]).then_inc(s2_ave, 1)
                        else:
                            v.wait_ge(s2_rc, a + 1)
                            v.tensor_scalar(ao_st[a % 2][:], pav[qc % 2][:, 0:128],
                                            rcp2[a % 2][:], None, ALU.mult,
                                            op1=ALU.bypass).then_inc(s2_ave, 1)
                    if u >= 1:
                        pu = u - 1
                        ph, pqh = pu // 2, pu % 2
                        v.wait_ge(s2_tr, 4 * (pu + 1))
                        if ph >= 2 and pqh == 0:
                            v.wait_ge(s2_asp, 16 * (ph - 1))
                        v.tensor_copy(at_st[ph % 2][:, pqh * 512:(pqh + 1) * 512],
                                      ptr[pu % 2][:]).then_inc(s2_tre, 1)
                pu = NU - 1
                ph, pqh = pu // 2, pu % 2
                v.wait_ge(s2_tr, 4 * (pu + 1))
                v.tensor_copy(at_st[ph % 2][:, pqh * 512:(pqh + 1) * 512],
                              ptr[pu % 2][:]).then_inc(s2_tre, 1)

        if "slab0" in p.taps:
            tap_slab0 = nc.dram_tensor("tap_slab0", [128, KC * 512], BF16, kind="ExternalOutput").ap()
            tap_vt0 = nc.dram_tensor("tap_vt0", [128, KC * 144], BF16, kind="ExternalOutput").ap()
            tap_nb0 = nc.dram_tensor("tap_nb0", [1, T], BF16, kind="ExternalOutput").ap()
            tsl = p.sem("tsl")
            with nc.Block() as blk:
                @blk.sync
                def _(sync):
                    sync.dma_start(out=tap_slab0, in_=slab[0][:]).then_inc(tsl, 16)
                    sync.dma_start(out=tap_vt0, in_=v_t[0][:]).then_inc(tsl, 16)
                    sync.dma_start(out=tap_nb0, in_=nb_st[0][:]).then_inc(tsl, 16)
                    sync.wait_ge(tsl, 48)

        if "attnT" in p.taps:
            tap_attnT = nc.dram_tensor("tap_attnT", [D, T], BF16, kind="ExternalOutput").ap()
            tat = p.sem("tat")
            with nc.Block() as blk:
                @blk.sync
                def _(sync):
                    sync.dma_start(out=tap_attnT, in_=attnT_d.ap()).then_inc(tat, 16)
                    sync.wait_ge(tat, 16)

        s2.close()

    nc.leave_named_scope("s2", _sid2, False)
    _sid3, _ = nc.enter_named_scope("s3", False)
    if "s3" in stages and level >= 8:
        s3a = ExitStack()
        woslab = [p.sb(f"s3wo{i}", [128, KC * 512], BF16, s3a) for i in range(2)]
        atile = p.sb("s3at", [128, TCH * KC * 128], BF16, s3a)
        ost = [p.sb(f"s3os{i}", [128, 512], F32, s3a) for i in range(2)]
        po = [p.ps(f"s3po{i}", [128, 512], F32, s3a) for i in range(2)]
        s3_wl = p.sem("s3_wl"); s3_al = p.sem("s3_al"); s3_pe = p.sem("s3_pe")
        s3_ev = p.sem("s3_ev"); s3_sp = p.sem("s3_sp")

        with nc.Block() as blk:
            @blk.sync
            def _(sync):
                sync.dma_start(
                    out=atile[:].rearrange("p (tc ec t) -> p tc ec t", ec=KC, t=128),
                    in_=attnT_d.ap().rearrange("(ec p) (tc t) -> p tc ec t", p=128, t=128)
                ).then_inc(s3_al, 16)

            @blk.gpsimd
            def _(g):
                for ob in range(TCH):
                    if ob >= 2:
                        g.wait_ge(s3_pe, 8 * (ob - 1))
                    g.dma_start(
                        out=woslab[ob % 2][:].rearrange("p (ec o) -> p ec o", o=512),
                        in_=woT[:, ob * 512:(ob + 1) * 512]
                        .rearrange("(ec p) o -> p ec o", p=128)).then_inc(s3_wl, 16)

            @blk.tensor
            def _(t):
                t.wait_ge(s3_al, 16)
                for ob in range(TCH):
                    t.wait_ge(s3_wl, 16 * (ob + 1))
                    for tc in range(TCH):
                        idx = ob * 8 + tc
                        if idx >= 2:
                            t.wait_ge(s3_ev, idx - 1)
                        for ec in range(KC):
                            ins = t.matmul(po[idx % 2][:],
                                           atile[:, (tc * KC + ec) * 128:(tc * KC + ec + 1) * 128],
                                           woslab[ob % 2][:, ec * 512:(ec + 1) * 512],
                                           start=(ec == 0), stop=(ec == KC - 1))
                        ins.then_inc(s3_pe, 1)

            @blk.vector
            def _(v):
                for idx in range(64):
                    v.wait_ge(s3_pe, idx + 1)
                    if idx >= 2:
                        v.wait_ge(s3_sp, 16 * (idx - 1))
                    v.tensor_copy(ost[idx % 2][:], po[idx % 2][:]).then_inc(s3_ev, 1)

            @blk.scalar
            def _(sc):
                for idx in range(64):
                    ob, tc = idx // 8, idx % 8
                    sc.wait_ge(s3_ev, idx + 1)
                    sc.dma_start(out=attno_d.ap()[tc * 128:(tc + 1) * 128,
                                                  ob * 512:(ob + 1) * 512],
                                 in_=ost[idx % 2][:]).then_inc(s3_sp, 16)
                sc.wait_ge(s3_sp, 16 * 64)
        s3a.close()

        s3b = ExitStack()
        aor = [p.sb(f"s3ao{i}", [128, D], F32, s3b) for i in range(2)]
        xr = [p.sb(f"s3xr{i}", [128, D], F32, s3b) for i in range(2)]
        x3 = [p.sb(f"s3x3{i}", [128, D], BF16, s3b) for i in range(2)]
        sq3 = p.sb("s3sq", [128, D], F32, s3b)
        xgf = p.sb("s3xgf", [128, 32], F32, s3b)
        xg = p.sb("s3xg", [128, 32], BF16, s3b)

        class LN3C:
            pass
        l3 = LN3C()
        l3.aggr = p.sb("s3ag", [128, 2], F32, s3b)
        l3.negm = p.sb("s3nm", [128, 1], F32, s3b)
        l3.vareps = p.sb("s3ve", [128, 1], F32, s3b)
        l3.rstd = p.sb("s3rs", [128, 1], F32, s3b)
        l3.mv = p.sb("s3mv", [128, 2], F32, s3b)
        pt3 = [p.ps(f"s3pt{i}", [128, 512], BF16, s3b) for i in range(2)]
        pw3 = p.ps("s3pw", [128, 128], F32, s3b)
        ptg = p.ps("s3ptg", [32, 128], BF16, s3b)

        s3_ld = p.sem("s3_ld"); s3_v2s = p.sem("s3_v2s"); s3_s2v = p.sem("s3_s2v")
        s3_x3 = p.sem("s3_x3"); s3_tp = p.sem("s3_tp"); s3_te = p.sem("s3_te")
        s3_xp = p.sem("s3_xp"); s3_tg = p.sem("s3_tg"); s3_tge = p.sem("s3_tge")
        s3_tgc = p.sem("s3_tgc")

        with nc.Block() as blk:
            @blk.sync
            def _(sync):
                for tc in range(TCH):
                    if tc >= 2:
                        sync.wait_ge(s3_x3, tc - 1)
                    sync.dma_start(out=aor[tc % 2][:],
                                   in_=attno_d.ap()[tc * 128:(tc + 1) * 128, :]).then_inc(s3_ld, 16)
                    sync.dma_start(out=xr[tc % 2][:],
                                   in_=x_in[tc * 128:(tc + 1) * 128, :]).then_inc(s3_ld, 16)
                    sync.wait_ge(s3_xp, tc + 1)
                    sync.dma_start(out=xp_d.ap()[tc * 128:(tc + 1) * 128, :],
                                   in_=aor[tc % 2][:]).then_inc(s3_ld, 16)

            @blk.vector
            def _(v):
                for tc in range(TCH):
                    v.wait_ge(s3_ld, 48 * tc + 32)
                    v.tensor_reduce(l3.aggr[:, 0:1], aor[tc % 2][:], AX, ALU.add)
                    v.tensor_tensor(sq3[:], aor[tc % 2][:], aor[tc % 2][:], ALU.mult)
                    v.tensor_reduce(l3.aggr[:, 1:2], sq3[:], AX, ALU.add)
                    v.tensor_scalar(l3.negm[:], l3.aggr[:, 0:1], -1.0 / D, None, ALU.mult, op1=ALU.bypass)
                    v.tensor_tensor(l3.mv[:, 0:1], l3.negm[:], l3.negm[:], ALU.mult)
                    v.tensor_scalar(l3.aggr[:, 1:2], l3.aggr[:, 1:2], 1.0 / D, None, ALU.mult, op1=ALU.bypass)
                    v.tensor_tensor(l3.aggr[:, 1:2], l3.aggr[:, 1:2], l3.mv[:, 0:1], ALU.subtract)
                    v.tensor_scalar(l3.vareps[:], l3.aggr[:, 1:2], EPS, None, ALU.add,
                                    op1=ALU.bypass).then_inc(s3_v2s, 1)
                    v.wait_ge(s3_s2v, 2 * tc + 1)
                    v.tensor_scalar(sq3[:], aor[tc % 2][:], l3.negm[:], l3.rstd[:], ALU.add, op1=ALU.mult)
                    v.tensor_tensor(aor[tc % 2][:], xr[tc % 2][:], sq3[:],
                                    ALU.add).then_inc(s3_xp, 1)
                    v.tensor_reduce(l3.aggr[:, 0:1], aor[tc % 2][:], AX, ALU.add)
                    v.tensor_tensor(sq3[:], aor[tc % 2][:], aor[tc % 2][:], ALU.mult)
                    v.tensor_reduce(l3.aggr[:, 1:2], sq3[:], AX, ALU.add)
                    v.tensor_scalar(l3.negm[:], l3.aggr[:, 0:1], -1.0 / D, None, ALU.mult, op1=ALU.bypass)
                    v.tensor_tensor(l3.mv[:, 0:1], l3.negm[:], l3.negm[:], ALU.mult)
                    v.tensor_scalar(l3.aggr[:, 1:2], l3.aggr[:, 1:2], 1.0 / D, None, ALU.mult, op1=ALU.bypass)
                    v.tensor_tensor(l3.aggr[:, 1:2], l3.aggr[:, 1:2], l3.mv[:, 0:1], ALU.subtract)
                    v.tensor_scalar(l3.vareps[:], l3.aggr[:, 1:2], EPS, None, ALU.add,
                                    op1=ALU.bypass).then_inc(s3_v2s, 1)
                    v.wait_ge(s3_s2v, 2 * tc + 2)
                    if tc >= 2:
                        v.wait_ge(s3_tp, (tc - 1) * 8)
                    v.tensor_scalar(x3[tc % 2][:], aor[tc % 2][:], l3.negm[:], l3.rstd[:],
                                    ALU.add, op1=ALU.mult).then_inc(s3_x3, 1)

            @blk.scalar
            def _(sc):
                for tc in range(TCH):
                    for j in range(2):
                        sc.wait_ge(s3_v2s, 2 * tc + j + 1)
                        sc.activation(l3.rstd[:], l3.vareps[:], AF.Ln)
                        sc.activation(l3.rstd[:], l3.rstd[:], AF.Exp, scale=-0.5).then_inc(s3_s2v, 1)
                    for g in range(8):
                        gi = tc * 8 + g
                        pt = pt3[g % 2]
                        sc.wait_ge(s3_tp, gi + 1)
                        for j in range(4):
                            dc = g * 4 + j
                            ins = sc.copy(xln3T[:, (dc * 8 + tc) * 128:(dc * 8 + tc + 1) * 128],
                                          pt[:, j * 128:(j + 1) * 128])
                        ins.then_inc(s3_te, 1)

            @blk.tensor
            def _(t):
                for tc in range(TCH):
                    for wi in range(150):
                        t.matmul(pw3[:], ident[:], ident[:], start=(wi == 0), stop=(wi == 149))
                    t.wait_ge(s3_x3, tc + 1)
                    for g in range(8):
                        gi = tc * 8 + g
                        pt = pt3[g % 2]
                        if gi >= 2:
                            t.wait_ge(s3_te, gi - 1)
                        for j in range(4):
                            dc = g * 4 + j
                            ins = t.transpose(pt[:, j * 128:(j + 1) * 128],
                                              x3[tc % 2][:, dc * 128:(dc + 1) * 128], ident[:])
                            if j == 3:
                                ins.then_inc(s3_tp, 1)
        pxg_st = ExitStack()
        pxg = [p.ps(f"s3pxg{i}", [32, 512], F32, pxg_st) for i in range(2)]
        s3_xg2 = p.sem("s3_xg2")
        with nc.Block() as blk:
            @blk.tensor
            def _(t):
                for th in range(2):
                    for dc in range(KC):
                        ins = t.matmul(pxg[th][0:32, :], oneh[:, 31 - dc:63 - dc],
                                       xln3T[:, (dc * 8 + th * 4) * 128:(dc * 8 + th * 4 + 4) * 128],
                                       start=(dc == 0), stop=(dc == KC - 1))
                    ins.then_inc(s3_xg2, 1)

            @blk.scalar
            def _(sc):
                for th in range(2):
                    sc.wait_ge(s3_xg2, th + 1)
                    sc.copy(xgT[0:32, th * 512:(th + 1) * 512], pxg[th][0:32, :])
        pxg_st.close()
        s3b.close()

    nc.leave_named_scope("s3", _sid3, False)
    _sid4, _ = nc.enter_named_scope("s4", False)
    if "s4" in stages and level >= 9:
        s3c = ExitStack()
        sf32 = p.sb("s3c_sf", [NCH, D], F32, s3c)
        sbf = p.sb("s3c_sb", [NCH, D], BF16, s3c)
        zn32 = p.sb("s3c_zn", [32, II], F32, s3c)
        z8 = p.sb("s3c_z8", [32, 1376 * 4], U8, s3c)
        z4u = p.sb("s3c_z4", [32, II], U8, s3c)
        zsb = p.sb("s3c_zs", [32, II], BF16, s3c)
        zn32d = p.sb("s3c_znd", [NCH, D], F32, s3c)
        z8d = p.sb("s3c_z8d", [NCH, 2048], U8, s3c)
        z4ud = p.sb("s3c_z4d", [NCH, D], U8, s3c)
        zsbd = p.sb("s3c_zsd", [NCH, D], BF16, s3c)
        s3c_ld = p.sem("s3c_ld"); s3c_dn = p.sem("s3c_dn")
        pwarm = p.ps("pwarm", [128, 128], F32, s3c)

        def prep_flat(src_ap, dst_d):
            with nc.Block() as blk:
                @blk.sync
                def _(sync):
                    sync.dma_start(out=sf32[:], in_=src_ap).then_inc(s3c_ld, 16)
                    sync.wait_ge(s3c_ld, p.cnt_s3c + 16)
                p.cnt_s3c += 16
            with nc.Block() as blk:
                @blk.tensor
                def _(t):
                    for i in range(400):
                        t.matmul(pwarm[:], ident[:], ident[:], start=(i == 0), stop=(i == 399))

                @blk.vector
                def _(v):
                    v.tensor_copy(sbf[:], sf32[:])
            with nc.Block() as blk:
                @blk.sync
                def _(sync):
                    sync.dma_start(out=dst_d.ap(), in_=sbf[:]).then_inc(s3c_ld, 16)
                    sync.wait_ge(s3c_ld, p.cnt_s3c + 16)
                p.cnt_s3c += 16

        def prep_zs(z_ap, s_nat_ap, dst_d, rows, nb):
            zz8 = z8 if rows == 32 else z8d
            zz4 = z4u if rows == 32 else z4ud
            zzn = zn32 if rows == 32 else zn32d
            zzs = zsb if rows == 32 else zsbd
            nel = nb * 2
            with nc.Block() as blk:
                @blk.sync
                def _(sync):
                    sync.dma_start(out=zz8[0:rows, 0:nb], in_=z_ap.bitcast(U8)).then_inc(s3c_ld, 16)
                    sync.dma_start(out=zzn[0:rows, 0:nel], in_=s_nat_ap).then_inc(s3c_ld, 16)
                    sync.wait_ge(s3c_ld, p.cnt_s3c + 32)
                p.cnt_s3c += 32
            with nc.Block() as blk:
                @blk.tensor
                def _(t):
                    for i in range(400):
                        t.matmul(pwarm[:], ident[:], ident[:], start=(i == 0), stop=(i == 399))

                @blk.vector
                def _(v):
                    v.tensor_scalar(zzn[0:rows, 0:nel], zzn[0:rows, 0:nel], -1.0, None,
                                    ALU.mult, op1=ALU.bypass)
                    zv = zz4[0:rows, 0:nel].rearrange("p (n two) -> p n two", two=2)
                    v.tensor_scalar(zv[:, :, 0], zz8[0:rows, 0:nb], 15, None,
                                    ALU.bitwise_and, op1=ALU.bypass)
                    v.tensor_scalar(zv[:, :, 1], zz8[0:rows, 0:nb], 4, None,
                                    ALU.logical_shift_right, op1=ALU.bypass)
                    v.tensor_tensor(zzs[0:rows, 0:nel], zz4[0:rows, 0:nel],
                                    zzn[0:rows, 0:nel], ALU.mult)
            with nc.Block() as blk:
                @blk.sync
                def _(sync):
                    sync.dma_start(out=dst_d.ap(), in_=zzs[0:rows, 0:nel]).then_inc(s3c_ld, 16)
                    sync.wait_ge(s3c_ld, p.cnt_s3c + 16)
                p.cnt_s3c += 16

        p.cnt_s3c = 0
        prep_flat(sg_flat, sgf_bf)
        prep_flat(su_flat, suf_bf)
        prep_flat(sd_nat, sdf_bf)
        prep_zs(zg, sg_nat, zs_g_d, 32, 5504)
        prep_zs(zu, su_nat, zs_u_d, 32, 5504)
        prep_zs(zd, sd_nat, zs_d_d, NCH, 2048)
        s3c.close()

        s4a = ExitStack()
        dpk8 = [p.sb(f"s4dp{i}", [128, 2048], U8, s4a) for i in range(2)]
        du8 = [p.sb(f"s4du{i}", [128, D], U8, s4a) for i in range(2)]
        srow = [p.sb(f"s4sr{i}", [1, D], BF16, s4a) for i in range(2)]
        wdst = [p.sb(f"s4wd{i}", [128, D], BF16, s4a) for i in range(2)]
        pbc = [p.ps(f"s4pb{i}", [128, 1024], F32, s4a) for i in range(4)]
        s4_ld = p.sem("s4_ld"); s4_un = p.sem("s4_un"); s4_bc = p.sem("s4_bc")
        s4_mu = p.sem("s4_mu"); s4_sp = p.sem("s4_sp")

        with nc.Block() as blk:
            @blk.sync
            def _(sync):
                for c in range(NCH):
                    if c >= 2:
                        sync.wait_ge(s4_un, 2 * (c - 1))
                    sync.dma_start(out=dpk8[c % 2][:],
                                   in_=dpk[c * 128:(c + 1) * 128, :].bitcast(U8)).then_inc(s4_ld, 16)
                    if c >= 2:
                        sync.wait_ge(s4_bc, 4 * (c - 1))
                    sync.dma_start(out=srow[c % 2][:],
                                   in_=sdf_bf.ap()[c:c + 1, :]).then_inc(s4_ld, 16)

            @blk.tensor
            def _(t):
                for c in range(NCH):
                    t.wait_ge(s4_ld, 32 * (c + 1))
                    for j in range(4):
                        b = (4 * c + j) % 4
                        if 4 * c + j >= 4:
                            t.wait_ge(s4_mu, 4 * c + j - 3)
                        for wi in range(8):
                            t.matmul(pbc[b][:, 0:128], ident[:], ident[:],
                                     start=(wi == 0), stop=(wi == 7))
                        for hh in range(2):
                            ins = t.matmul(pbc[b][:, hh * 512:(hh + 1) * 512],
                                           ones1[:],
                                           srow[c % 2][0:1, j * 1024 + hh * 512:j * 1024 + (hh + 1) * 512],
                                           start=True, stop=True)
                        ins.then_inc(s4_bc, 1)

            @blk.vector
            def _(v):
                for c in range(NCH):
                    v.wait_ge(s4_ld, 32 * c + 16)
                    dv = du8[c % 2][:].rearrange("p (n two) -> p n two", two=2)
                    v.tensor_scalar(dv[:, :, 0], dpk8[c % 2][:], 15, None,
                                    ALU.bitwise_and, op1=ALU.bypass)
                    v.tensor_scalar(dv[:, :, 1], dpk8[c % 2][:], 4, None,
                                    ALU.logical_shift_right, op1=ALU.bypass).then_inc(s4_un, 2)
                    for j in range(4):
                        v.wait_ge(s4_bc, 4 * c + j + 1)
                        if c >= 2:
                            v.wait_ge(s4_sp, 16 * (c - 1))
                        v.tensor_tensor(wdst[c % 2][:, j * 1024:(j + 1) * 1024],
                                        du8[c % 2][:, j * 1024:(j + 1) * 1024],
                                        pbc[j][:], ALU.mult).then_inc(s4_mu, 1)

            @blk.scalar
            def _(sc):
                for c in range(NCH):
                    sc.wait_ge(s4_mu, 4 * (c + 1))
                    sc.dma_start(out=wd_bf.ap()[c, :, :],
                                 in_=wdst[c % 2][:]).then_inc(s4_sp, 16)
                sc.wait_ge(s4_sp, 16 * NCH)
        s4a.close()

        zq = ExitStack()
        zgq = p.sb("s4zg", [32, 22 * 128], BF16, zq)
        zuq = p.sb("s4zu", [32, 22 * 128], BF16, zq)
        zdq = p.sb("s4zd", [32, D], BF16, zq)
        hgT = p.sb("s4hg", [32, T], BF16, zq)
        s4b = {n: p.sem("s4b_" + n) for n in
               ["zl", "pk", "un", "bc", "mu", "pe", "si", "hm", "hg", "hge",
                "wl", "pd", "de", "ol", "os"]}
        CNT = {n: 0 for n in s4b}

        def quarter(qi, c0, cn):
            hst = ExitStack()
            hsl = p.sb(f"s4h{qi}", [128, cn * 1024], BF16, hst)
            gst = ExitStack()
            dpg = [p.sb(f"s4gp{qi}0", [128, 2048], U8, gst)] * 2
            dpu = [p.sb(f"s4up{qi}0", [128, 2048], U8, gst)] * 2
            ug8 = [p.sb(f"s4g8{qi}{i}", [128, D], U8, gst) for i in range(2)]
            uu8 = [p.sb(f"s4u8{qi}{i}", [128, D], U8, gst) for i in range(2)]
            wtg = [p.sb(f"s4wg{qi}{i}", [128, D], BF16, gst) for i in range(2)]
            wtu = [p.sb(f"s4wu{qi}{i}", [128, D], BF16, gst) for i in range(2)]
            srg = [p.sb(f"s4sg{qi}0", [1, D], BF16, gst)] * 2
            sru = [p.sb(f"s4su{qi}0", [1, D], BF16, gst)] * 2
            hss = p.sb(f"s4hs{qi}", [128, 512], BF16, gst)
            gps = ExitStack()
            pg = [p.ps(f"s4pg{qi}{i}", [128, 512], F32, gps) for i in range(2)]
            pu = [p.ps(f"s4pu{qi}{i}", [128, 512], F32, gps) for i in range(2)]
            pb2 = [p.ps(f"s4pb{qi}{i}", [128, 1024], F32, gps) for i in range(2)]

            with nc.Block() as blk:
                @blk.sync
                def _(sync):
                    sync.dma_start(out=zgq[:, 0:cn * 128],
                                   in_=zs_g_d.ap()[:, c0 * 128:(c0 + cn) * 128]).then_inc(s4b["zl"], 16)
                    sync.dma_start(out=zuq[:, 0:cn * 128],
                                   in_=zs_u_d.ap()[:, c0 * 128:(c0 + cn) * 128]).then_inc(s4b["zl"], 16)
                    sync.dma_start(out=zdq[0:cn, :],
                                   in_=zs_d_d.ap()[c0:c0 + cn, :]).then_inc(s4b["zl"], 16)
                    CNT["zl"] += 48
                    for ci in range(cn):
                        c = c0 + ci
                        if ci >= 1:
                            sync.wait_ge(s4b["un"], CNT["un"] + 4 * ci)
                        sync.dma_start(out=dpg[ci % 2][:],
                                       in_=gpk[c, :, :].bitcast(U8)).then_inc(s4b["pk"], 16)
                        sync.dma_start(out=dpu[ci % 2][:],
                                       in_=upk[c, :, :].bitcast(U8)).then_inc(s4b["pk"], 16)
                        if ci >= 1:
                            sync.wait_ge(s4b["bc"], CNT["bc"] + 8 * ci)
                        sync.dma_start(out=srg[ci % 2][:],
                                       in_=sgf_bf.ap()[c:c + 1, :]).then_inc(s4b["pk"], 16)
                        sync.dma_start(out=sru[ci % 2][:],
                                       in_=suf_bf.ap()[c:c + 1, :]).then_inc(s4b["pk"], 16)

                @blk.tensor
                def _(t):
                    t.wait_ge(s4b["zl"], CNT["zl"])
                    for ci in range(cn):
                        t.wait_ge(s4b["pk"], CNT["pk"] + 64 * (ci + 1))
                        for m, sr in [(0, srg), (1, sru)]:
                            for j in range(4):
                                bi = 8 * ci + 4 * m + j
                                if bi >= 2:
                                    t.wait_ge(s4b["mu"], CNT["mu"] + bi - 1)
                                for hh in range(2):
                                    ins = t.matmul(pb2[bi % 2][:, hh * 512:(hh + 1) * 512],
                                                   ones1[:],
                                                   sr[ci % 2][0:1, j * 1024 + hh * 512:
                                                              j * 1024 + (hh + 1) * 512],
                                                   start=True, stop=True)
                                ins.then_inc(s4b["bc"], 1)
                        for m, wt, zz in [(0, wtg, zgq), (1, wtu, zuq)]:
                            t.wait_ge(s4b["mu"], CNT["mu"] + 8 * ci + 4 * (m + 1))
                            pp = pg if m == 0 else pu
                            for th in range(2):
                                if ci >= 1 and m == 0:
                                    t.wait_ge(s4b["si"], CNT["si"] + 2 * (ci - 1) + th + 1)
                                if ci >= 1 and m == 1:
                                    t.wait_ge(s4b["hm"], CNT["hm"] + 2 * (ci - 1) + th + 1)
                                for kc in range(KC):
                                    ins = t.matmul(pp[th][:],
                                                   wt[ci % 2][:, kc * 128:(kc + 1) * 128],
                                                   xln3T[:, (kc * 8 + th * 4) * 128:
                                                         (kc * 8 + th * 4 + 4) * 128],
                                                   start=(kc == 0), stop=False)
                                ins = t.matmul(pp[th][:], zz[:, ci * 128:(ci + 1) * 128],
                                               xgT[0:32, th * 512:(th + 1) * 512],
                                               start=False, stop=True)
                                ins.then_inc(s4b["pe"], 1)
                    phg = [pb2[0][0:32, 0:512], pb2[1][0:32, 0:512]]
                    t.wait_ge(s4b["hm"], CNT["hm"] + 2 * cn)
                    for ci in range(cn):
                        for th in range(2):
                            ins = t.matmul(phg[th][0:22, :], oneh[:, 31 - ci:53 - ci],
                                           hsl[:, ci * 1024 + th * 512:ci * 1024 + (th + 1) * 512],
                                           start=(ci == 0), stop=(ci == cn - 1))
                            if ci == cn - 1:
                                ins.then_inc(s4b["hg"], 1)

                @blk.vector
                def _(v):
                    for ci in range(cn):
                        v.wait_ge(s4b["pk"], CNT["pk"] + 64 * ci + 32)
                        for u8s, dps in [(ug8, dpg), (uu8, dpu)]:
                            uv = u8s[ci % 2][:].rearrange("p (n two) -> p n two", two=2)
                            v.tensor_scalar(uv[:, :, 0], dps[ci % 2][:], 15, None,
                                            ALU.bitwise_and, op1=ALU.bypass)
                            v.tensor_scalar(uv[:, :, 1], dps[ci % 2][:], 4, None,
                                            ALU.logical_shift_right, op1=ALU.bypass).then_inc(s4b["un"], 2)
                        for m in range(2):
                            u8s = ug8 if m == 0 else uu8
                            wt = wtg if m == 0 else wtu
                            for j in range(4):
                                bi = 8 * ci + 4 * m + j
                                v.wait_ge(s4b["bc"], CNT["bc"] + bi + 1)
                                if ci >= 2 and m == 0 and j == 0:
                                    v.wait_ge(s4b["pe"], CNT["pe"] + 4 * (ci - 1))
                                v.tensor_tensor(wt[ci % 2][:, j * 1024:(j + 1) * 1024],
                                                u8s[ci % 2][:, j * 1024:(j + 1) * 1024],
                                                pb2[bi % 2][:], ALU.mult).then_inc(s4b["mu"], 1)
                        for th in range(2):
                            v.wait_ge(s4b["pe"], CNT["pe"] + 4 * ci + 3 + th)
                            v.wait_ge(s4b["si"], CNT["si"] + 2 * ci + th + 1)
                            v.tensor_tensor(hsl[:, ci * 1024 + th * 512:ci * 1024 + (th + 1) * 512],
                                            pu[th][:], hsil[th][:], ALU.mult).then_inc(s4b["hm"], 1)

                @blk.scalar
                def _(sc):
                    for ci in range(cn):
                        for th in range(2):
                            sc.wait_ge(s4b["pe"], CNT["pe"] + 4 * ci + th + 1)
                            if ci >= 1:
                                sc.wait_ge(s4b["hm"], CNT["hm"] + 2 * (ci - 1) + th + 1)
                            sc.activation(hsil[th][:], pg[th][:], AF.Silu).then_inc(s4b["si"], 1)
                    sc.wait_ge(s4b["hg"], CNT["hg"] + 2)
                    sc.copy(hgT[0:cn, 0:512], pb2[0][0:cn, 0:512])
                    ins = sc.copy(hgT[0:cn, 512:1024], pb2[1][0:cn, 0:512])
                    ins.then_inc(s4b["hge"], 1)

            CNT["zl"] += 0
            CNT["pk"] += 64 * cn
            CNT["un"] += 4 * cn
            CNT["bc"] += 8 * cn
            CNT["mu"] += 8 * cn
            CNT["pe"] += 4 * cn
            CNT["si"] += 2 * cn
            CNT["hm"] += 2 * cn
            CNT["hg"] += 2
            CNT["hge"] += 1
            gst.close()
            gps.close()

            dst_ = ExitStack()
            wdsl = [p.sb(f"s4wl{qi}{i}", [128, cn * 512], BF16, dst_) for i in range(2)]
            odst = [p.sb(f"s4od{qi}{i}", [128, 512], BF16, dst_) for i in range(2)]
            opv = [p.sb(f"s4ov{qi}{i}", [128, 512], BF16, dst_) for i in range(2)]
            dps_ = ExitStack()
            pd = [p.ps(f"s4pd{qi}{i}", [128, 512], F32, dps_) for i in range(2)]

            with nc.Block() as blk:
                @blk.sync
                def _(sync):
                    for ob in range(TCH):
                        if ob >= 2:
                            sync.wait_ge(s4b["pd"], CNT["pd"] + 8 * (ob - 1))
                        for ci in range(cn):
                            sync.dma_start(out=wdsl[ob % 2][:, ci * 512:(ci + 1) * 512],
                                           in_=wd_bf.ap()[c0 + ci, :, ob * 512:(ob + 1) * 512]
                                           ).then_inc(s4b["wl"], 16)
                        if qi > 0:
                            for tc in range(TCH):
                                idx = ob * 8 + tc
                                if idx >= 2:
                                    sync.wait_ge(s4b["de"], CNT["de"] + idx - 1)
                                sync.dma_start(out=opv[idx % 2][:],
                                               in_=op_d.ap()[tc * 128:(tc + 1) * 128,
                                                             ob * 512:(ob + 1) * 512]
                                               ).then_inc(s4b["ol"], 16)

                @blk.tensor
                def _(t):
                    for ob in range(TCH):
                        t.wait_ge(s4b["wl"], CNT["wl"] + 16 * cn * (ob + 1))
                        for tc in range(TCH):
                            idx = ob * 8 + tc
                            if idx >= 2:
                                t.wait_ge(s4b["de"], CNT["de"] + idx - 1)
                            for ci in range(cn):
                                t.matmul(pd[idx % 2][:],
                                         hsl[:, ci * 1024 + tc * 128:ci * 1024 + tc * 128 + 128],
                                         wdsl[ob % 2][:, ci * 512:(ci + 1) * 512],
                                         start=(ci == 0), stop=False)
                            ins = t.matmul(pd[idx % 2][:], hgT[0:cn, tc * 128:(tc + 1) * 128],
                                           zdq[0:cn, ob * 512:(ob + 1) * 512],
                                           start=False, stop=True)
                            ins.then_inc(s4b["pd"], 1)

                @blk.vector
                def _(v):
                    for ob in range(TCH):
                        for tc in range(TCH):
                            idx = ob * 8 + tc
                            v.wait_ge(s4b["pd"], CNT["pd"] + idx + 1)
                            if idx >= 2:
                                v.wait_ge(s4b["os"], CNT["os"] + 16 * (idx - 1))
                            if qi == 0:
                                v.tensor_copy(odst[idx % 2][:],
                                              pd[idx % 2][:]).then_inc(s4b["de"], 1)
                            else:
                                v.wait_ge(s4b["ol"], CNT["ol"] + 16 * (idx + 1))
                                v.tensor_tensor(odst[idx % 2][:], pd[idx % 2][:],
                                                opv[idx % 2][:], ALU.add).then_inc(s4b["de"], 1)

                @blk.scalar
                def _(sc):
                    for ob in range(TCH):
                        for tc in range(TCH):
                            idx = ob * 8 + tc
                            sc.wait_ge(s4b["de"], CNT["de"] + idx + 1)
                            sc.dma_start(out=op_d.ap()[tc * 128:(tc + 1) * 128,
                                                       ob * 512:(ob + 1) * 512],
                                         in_=odst[idx % 2][:]).then_inc(s4b["os"], 16)
                    sc.wait_ge(s4b["os"], CNT["os"] + 16 * 64)

            CNT["wl"] += 16 * cn * TCH
            CNT["pd"] += 64
            CNT["de"] += 64
            CNT["os"] += 16 * 64
            if qi > 0:
                CNT["ol"] += 16 * 64
            dps_.close()
            dst_.close()
            hst.close()

        hsil = [p.sb(f"s4hsil{i}", [128, 512], BF16, zq) for i in range(2)]
        for qi, (c0, cn) in enumerate(QUARTERS):
            quarter(qi, c0, cn)
        zq.close()

    if "dump" in p.taps:
        tap_attno = nc.dram_tensor("tap_attno", [T, D], F32, kind="ExternalOutput").ap()
        tap_xp = nc.dram_tensor("tap_xp", [T, D], F32, kind="ExternalOutput").ap()
        tdmp = p.sem("tdmp")
        with nc.Block() as blk:
            @blk.sync
            def _(sync):
                sync.dma_start(out=tap_attno, in_=attno_d.ap()).then_inc(tdmp, 16)
                sync.dma_start(out=tap_xp, in_=xp_d.ap()).then_inc(tdmp, 16)
                sync.wait_ge(tdmp, 32)

    if "dump2" in p.taps:
        tap_op = nc.dram_tensor("tap_op", [T, D], BF16, kind="ExternalOutput").ap()
        tap_wd0 = nc.dram_tensor("tap_wd0", [128, D], BF16, kind="ExternalOutput").ap()
        tap_x3T = nc.dram_tensor("tap_x3T", [128, KC * T], BF16, kind="ExternalOutput").ap()
        tap_xgT = nc.dram_tensor("tap_xgT", [32, T], BF16, kind="ExternalOutput").ap()
        tdmp2 = p.sem("tdmp2")
        with nc.Block() as blk:
            @blk.sync
            def _(sync):
                sync.dma_start(out=tap_op, in_=op_d.ap()).then_inc(tdmp2, 16)
                sync.dma_start(out=tap_wd0, in_=wd_bf.ap()[0, :, :]).then_inc(tdmp2, 16)
                sync.wait_ge(tdmp2, 32)

            @blk.vector
            def _(v):
                pass

        with nc.Block() as blk:
            @blk.sync
            def _(sync):
                sync.dma_start(out=tap_x3T, in_=xln3T[:]).then_inc(tdmp2, 16)
                sync.dma_start(out=tap_xgT, in_=xgT[:]).then_inc(tdmp2, 16)
                sync.wait_ge(tdmp2, 64)

    nc.leave_named_scope("s4", _sid4, False)
    _sid5, _ = nc.enter_named_scope("s5", False)
    if "s5" in stages and level >= 10:
        s5 = ExitStack()
        opr = [p.sb(f"s5op{i}", [128, D], BF16, s5) for i in range(2)]
        xpr = [p.sb(f"s5xp{i}", [128, D], F32, s5) for i in range(2)]
        sq5 = p.sb("s5sq", [128, D], F32, s5)
        ln5 = p.sb("s5ln", [128, D], F32, s5)

        class L5:
            pass
        l5 = L5()
        l5.aggr = p.sb("s5ag", [128, 2], F32, s5)
        l5.negm = p.sb("s5nm", [128, 1], F32, s5)
        l5.vareps = p.sb("s5ve", [128, 1], F32, s5)
        l5.rstd = p.sb("s5rs", [128, 1], F32, s5)
        l5.mv = p.sb("s5mv", [128, 1], F32, s5)
        s5_ld = p.sem("s5_ld"); s5_v2s = p.sem("s5_v2s"); s5_s2v = p.sem("s5_s2v")
        s5_fin = p.sem("s5_fin"); s5_st = p.sem("s5_st")

        with nc.Block() as blk:
            @blk.sync
            def _(sync):
                for tc in range(TCH):
                    if tc >= 2:
                        sync.wait_ge(s5_fin, tc - 1)
                    sync.dma_start(out=opr[tc % 2][:],
                                   in_=op_d.ap()[tc * 128:(tc + 1) * 128, :]).then_inc(s5_ld, 16)
                    sync.dma_start(out=xpr[tc % 2][:],
                                   in_=xp_d.ap()[tc * 128:(tc + 1) * 128, :]).then_inc(s5_ld, 16)
                    sync.wait_ge(s5_fin, tc + 1)
                    sync.dma_start(out=out[tc * 128:(tc + 1) * 128, :],
                                   in_=xpr[tc % 2][:]).then_inc(s5_st, 16)
                sync.wait_ge(s5_st, 16 * TCH)

            @blk.vector
            def _(v):
                for tc in range(TCH):
                    v.wait_ge(s5_ld, 32 * (tc + 1))
                    v.tensor_reduce(l5.aggr[:, 0:1], opr[tc % 2][:], AX, ALU.add)
                    v.tensor_tensor(sq5[:], opr[tc % 2][:], opr[tc % 2][:], ALU.mult)
                    v.tensor_reduce(l5.aggr[:, 1:2], sq5[:], AX, ALU.add)
                    v.tensor_scalar(l5.negm[:], l5.aggr[:, 0:1], -1.0 / D, None, ALU.mult, op1=ALU.bypass)
                    v.tensor_tensor(l5.mv[:], l5.negm[:], l5.negm[:], ALU.mult)
                    v.tensor_scalar(l5.aggr[:, 1:2], l5.aggr[:, 1:2], 1.0 / D, None, ALU.mult, op1=ALU.bypass)
                    v.tensor_tensor(l5.aggr[:, 1:2], l5.aggr[:, 1:2], l5.mv[:], ALU.subtract)
                    v.tensor_scalar(l5.vareps[:], l5.aggr[:, 1:2], EPS, None, ALU.add,
                                    op1=ALU.bypass).then_inc(s5_v2s, 1)
                    v.wait_ge(s5_s2v, tc + 1)
                    v.tensor_scalar(ln5[:], opr[tc % 2][:], l5.negm[:], l5.rstd[:], ALU.add, op1=ALU.mult)
                    v.tensor_tensor(xpr[tc % 2][:], xpr[tc % 2][:], ln5[:],
                                    ALU.add).then_inc(s5_fin, 1)

            @blk.scalar
            def _(sc):
                for tc in range(TCH):
                    sc.wait_ge(s5_v2s, tc + 1)
                    sc.activation(l5.rstd[:], l5.vareps[:], AF.Ln)
                    sc.activation(l5.rstd[:], l5.rstd[:], AF.Exp, scale=-0.5).then_inc(s5_s2v, 1)
        s5.close()
    nc.leave_named_scope("s5", _sid5, False)

    return nc


def _prep(inputs):
    gq = np.ascontiguousarray(np.asarray(inputs["L0_ffn_gate_qweight"]))
    uq = np.ascontiguousarray(np.asarray(inputs["L0_ffn_up_qweight"]))
    dq = np.ascontiguousarray(np.asarray(inputs["L0_ffn_down_qweight"]))
    sg = np.ascontiguousarray(np.asarray(inputs["L0_ffn_gate_scales"]))
    su = np.ascontiguousarray(np.asarray(inputs["L0_ffn_up_scales"]))
    sd = np.ascontiguousarray(np.asarray(inputs["L0_ffn_down_scales"]))

    def retile(q):
        u8 = q.view(np.uint8).reshape(KC, 128, NCH, 64)
        return np.ascontiguousarray(u8.transpose(2, 1, 0, 3)).reshape(NCH, 128, 2048).view(np.int32)

    def sflat(s):
        return np.ascontiguousarray(s.reshape(KC, NCH, 128).transpose(1, 0, 2)).reshape(NCH, D)

    oneh = np.zeros((128, 64), dtype=ml_dtypes.bfloat16)
    oneh[:, 31] = 1.0
    shared = {
        "wqT": np.ascontiguousarray(np.asarray(inputs["L0_attn_w_q"]).T),
        "wkT": np.ascontiguousarray(np.asarray(inputs["L0_attn_w_k"]).T),
        "wvT": np.ascontiguousarray(np.asarray(inputs["L0_attn_w_v"]).T),
        "woT": np.ascontiguousarray(np.asarray(inputs["L0_attn_w_o"]).T),
        "gpk": retile(gq), "upk": retile(uq),
        "dpk": dq,
        "sg_flat": sflat(sg), "su_flat": sflat(su),
        "sg_nat": sg, "su_nat": su, "sd_nat": sd,
        "zg": np.ascontiguousarray(np.asarray(inputs["L0_ffn_gate_qzeros"])),
        "zu": np.ascontiguousarray(np.asarray(inputs["L0_ffn_up_qzeros"])),
        "zd": np.ascontiguousarray(np.asarray(inputs["L0_ffn_down_qzeros"])),
        "c_id": np.eye(128, dtype=ml_dtypes.bfloat16),
        "c_oneh": oneh,
        "c_ones1": np.ones((1, 128), dtype=ml_dtypes.bfloat16),
    }
    x = np.asarray(inputs["L0_attn_x"])
    in_maps = []
    for c in range(8):
        m = dict(shared)
        m["x_in"] = np.ascontiguousarray(x[c // 4, (c % 4) * T:(c % 4 + 1) * T, :])
        in_maps.append(m)
    return in_maps


def kernel(**inputs):
    in_maps = _prep(inputs)
    nc = build()
    res = run_bass_kernel_spmd(nc, in_maps, core_ids=list(range(8)))
    x = np.asarray(inputs["L0_attn_x"])
    outp = np.empty((2, S, D), dtype=np.float32)
    for c in range(8):
        outp[c // 4, (c % 4) * T:(c % 4 + 1) * T, :] = res.results[c]["out"]
    return outp
